# version 02
# baseline (speedup 1.0000x reference)
import contextlib
import os as _os
import numpy as np
import ml_dtypes
import concourse.bass as bass
import concourse.mybir as mybir
from concourse.bass_utils import run_bass_kernel_spmd
from concourse.alu_op_type import AluOpType as OP

F32 = mybir.dt.float32
BF16 = mybir.dt.bfloat16
AF = mybir.ActivationFunctionType
AX = mybir.AxisListType
bfdt = ml_dtypes.bfloat16

TSEQ, DM = 4096, 1024
NBLK, BLK = 8, 512
NT = 32
OWN_BLOCKS = [[0, 3, 4, 7], [1, 2, 5, 6]]
PAD_OTHER = [1, 2, 3, 4]
NEG = -1.0e30
EPS = 1e-5


class Ctr:
    def __init__(self, h):
        self.h = h
        self.n = 0


class Bld:
    ENGS = ("pe", "act", "dve", "pool", "sp")

    def __init__(self):
        self.q = {e: [] for e in self.ENGS}

    def emit(self, eng, fn, inc=None):
        if inc is not None:
            incs = inc if isinstance(inc, list) else [inc]
            vals = []
            for s, k in incs:
                s.n += k
                vals.append((s.h, k))

            def run(e, fn=fn, vals=vals):
                ins = fn(e)
                for h, k in vals:
                    ins = ins.then_inc(h, k)
            self.q[eng].append(run)
        else:
            self.q[eng].append(fn)

    def wait(self, eng, sem, val=None):
        v = sem.n if val is None else val
        if v <= 0:
            return
        self.q[eng].append(lambda e, h=sem.h, v=v: e.wait_ge(h, v))


def _prep_core_inputs(inp, bb, g):
    f32 = np.float32
    own = OWN_BLOCKS[g]
    other = [blk for blk in range(NBLK) if blk not in own]
    rows = np.concatenate([np.arange(blk * BLK, (blk + 1) * BLK) for blk in own + other])

    d = {"x_perm": np.ascontiguousarray(np.asarray(inp["L0_attn_x"][bb], dtype=f32)[rows])}

    Wqkv = np.asarray(inp["L0_attn_Wqkv"], dtype=f32)
    bqkv = np.asarray(inp["L0_attn_bqkv"], dtype=f32)
    idx = []
    for part in range(3):
        for h in range(16):
            idx.extend(range(h * 192 + part * 64, h * 192 + part * 64 + 64))
    idx = np.array(idx)
    d["wqkvT"] = np.ascontiguousarray(Wqkv[idx].T)
    bq = bqkv[idx]
    d["bqkv_c"] = np.ascontiguousarray(bq[:2048].reshape(16, 128).T)
    d["bqkvV_b"] = np.ascontiguousarray(np.tile(bq[2048:], (128, 1)))
    d["woT"] = np.ascontiguousarray(np.asarray(inp["L0_attn_Wo"], f32).T)
    d["bo_b"] = np.ascontiguousarray(np.tile(np.asarray(inp["L0_attn_bo"], f32), (128, 1)))

    d["ln1w_c"] = np.ascontiguousarray(np.asarray(inp["L0_ffn_ln1_w"], f32).reshape(8, 128).T)
    d["ln2w_c"] = np.ascontiguousarray(np.asarray(inp["L0_ffn_ln2_w"], f32).reshape(8, 128).T)
    d["kvdT"] = np.ascontiguousarray(np.asarray(inp["L0_ffn_kv_d"], f32).T)
    d["qdT"] = np.ascontiguousarray(np.asarray(inp["L0_ffn_q_d"], f32).T)
    k_u = np.asarray(inp["L0_ffn_k_u"], f32)
    q_u = np.asarray(inp["L0_ffn_q_u"], f32)
    nope = np.concatenate([np.arange(h * 64 + 32, h * 64 + 64) for h in range(16)])
    d["kuT"] = np.ascontiguousarray(k_u[nope].T)
    d["quT"] = np.ascontiguousarray(q_u[nope].T)
    d["vuT"] = np.ascontiguousarray(np.asarray(inp["L0_ffn_v_u"], f32).T)

    rq = np.asarray(inp["L0_ffn_rope_q_w"], f32)
    rk = np.asarray(inp["L0_ffn_rope_k_w"], f32)
    swap = np.array(sum([list(range(h * 32 + 16, h * 32 + 32)) +
                         list(range(h * 32, h * 32 + 16)) for h in range(16)], []),
                    dtype=np.int64)
    d["rqT"] = np.ascontiguousarray(rq.T)
    d["rqsT"] = np.ascontiguousarray(rq[swap].T)
    d["rkT"] = np.ascontiguousarray(rk.T)
    d["rksT"] = np.ascontiguousarray(rk[swap].T)
    d["owT"] = np.ascontiguousarray(np.asarray(inp["L0_ffn_o_w"], f32).T)
    d["routT"] = np.ascontiguousarray(np.asarray(inp["L0_ffn_router_w"], f32).T)
    d["routb_b"] = np.ascontiguousarray(np.tile(np.asarray(inp["L0_ffn_routing_bias"], f32),
                                                (128, 1)))
    gate = np.concatenate([np.asarray(inp["L0_ffn_shared_gate"], f32),
                           np.asarray(inp["L0_ffn_routed_gate"], f32)], axis=0)
    up = np.concatenate([np.asarray(inp["L0_ffn_shared_up"], f32),
                         np.asarray(inp["L0_ffn_routed_up"], f32)], axis=0)
    down = np.concatenate([np.asarray(inp["L0_ffn_shared_down"], f32),
                           np.asarray(inp["L0_ffn_routed_down"], f32)], axis=0)
    d["gateT"] = np.ascontiguousarray(np.transpose(gate, (0, 2, 1)))
    d["upT"] = np.ascontiguousarray(np.transpose(up, (0, 2, 1)))
    d["downT"] = np.ascontiguousarray(np.transpose(down, (0, 2, 1)))

    inv_freq = (1.0 / (10000.0 ** (np.arange(0, 32, 2, dtype=f32) / f32(32.0)))).astype(f32)
    tpos = np.arange(TSEQ, dtype=f32)
    emb = np.concatenate([tpos[:, None] * inv_freq[None, :]] * 2, axis=-1)
    cos = np.cos(emb).astype(f32)
    sin = np.sin(emb).astype(f32)
    sgn = np.where(np.arange(32) < 16, -1.0, 1.0).astype(f32)
    cosk = np.tile(cos[rows].T, (4, 1))
    sink = np.tile((sin[rows] * sgn).T, (4, 1))
    d["cosk"] = np.ascontiguousarray(cosk)
    d["sink"] = np.ascontiguousarray(sink)
    d["cosq"] = np.ascontiguousarray(cosk[:, :2048] * f32(0.125))
    d["sinq"] = np.ascontiguousarray(sink[:, :2048] * f32(0.125))

    j = np.arange(BLK)
    diag = np.empty((128, 4 * BLK), dtype=f32)
    for cc in range(4):
        kk = cc * 128 + np.arange(128)
        diag[:, cc * BLK:(cc + 1) * BLK] = np.where(kk[:, None] <= j[None, :], 0.0, NEG)
    d["mask_diag"] = np.ascontiguousarray(diag)
    hide = np.zeros((1, 4), dtype=f32)
    for i, blk in enumerate(own):
        if sum(1 for ob in other if ob < blk) < PAD_OTHER[i]:
            hide[0, i] = NEG
    d["hide_b"] = np.ascontiguousarray(np.tile(hide, (128, 1)))
    eh = np.zeros((128, 2), dtype=f32)
    eh[0:64, 0] = 1.0
    eh[64:128, 1] = 1.0
    import ml_dtypes as _mld
    d["ehost"] = eh.astype(_mld.bfloat16)
    sel2 = np.zeros((2, 256), dtype=f32)
    sel2[0, 0:128] = 1.0
    sel2[1, 128:256] = 1.0
    d["sel2"] = sel2.astype(_mld.bfloat16)
    return d, rows


def build_program(taps=(), stop_after=None):
    nc = bass.Bass("TRN2", target_bir_lowering=False, debug=False,
                   detect_race_conditions=False)
    b = Bld()
    es = contextlib.ExitStack()

    DI = {}
    for name, shape in [
        ("x_perm", [TSEQ, DM]), ("wqkvT", [DM, 3072]), ("bqkv_c", [128, 16]),
        ("bqkvV_b", [128, DM]), ("woT", [DM, DM]), ("bo_b", [128, DM]),
        ("ln1w_c", [128, 8]), ("ln2w_c", [128, 8]),
        ("kvdT", [DM, 256]), ("qdT", [DM, 256]), ("kuT", [256, 512]),
        ("quT", [256, 512]), ("vuT", [256, DM]),
        ("rqT", [256, 512]), ("rqsT", [256, 512]), ("rkT", [DM, 512]), ("rksT", [DM, 512]),
        ("owT", [DM, DM]), ("routT", [DM, 7]), ("routb_b", [128, 7]),
        ("gateT", [8, DM, 2048]), ("upT", [8, DM, 2048]), ("downT", [8, 2048, DM]),
        ("cosk", [128, TSEQ]), ("sink", [128, TSEQ]),
        ("cosq", [128, 2048]), ("sinq", [128, 2048]),
        ("mask_diag", [128, 4 * BLK]), ("hide_b", [128, 4]),
    ]:
        DI[name] = nc.dram_tensor(name, shape, F32, kind="ExternalInput").ap()
    DI["ehost"] = nc.dram_tensor("ehost", [128, 2], BF16, kind="ExternalInput").ap()
    DI["sel2"] = nc.dram_tensor("sel2", [2, 256], BF16, kind="ExternalInput").ap()
    out_d = nc.dram_tensor("out", [2048, DM], F32, kind="ExternalOutput").ap()

    DS = {}
    park_shapes = {
        "qt_park": ([DM, TSEQ], BF16), "kt_park": ([DM, TSEQ], BF16),
        "att_park": ([DM, TSEQ], BF16), "qt2_park": ([DM, 2048], BF16),
        "kt2_park": ([DM, TSEQ], BF16), "x1_park": ([2048, DM], F32),
        "f_park": ([2048, DM], F32),
    }
    for name, (shape, dt) in park_shapes.items():
        DS[name] = nc.dram_tensor(name, shape, dt, kind="Internal").ap()
    dbg = {}
    for name in taps:
        if name not in park_shapes:
            continue
        shp, dt = park_shapes[name]
        dbg[name] = nc.dram_tensor("dbg_" + name, shp, dt, kind="ExternalOutput").ap()

    def sbuf(name, shape, dt):
        return es.enter_context(nc.sbuf_tensor(name, list(shape), dt))

    def mksem(name):
        return Ctr(es.enter_context(nc.semaphore(name)))

    STR = {}
    for snm, q in [("x", "sp"), ("t0", "sp"), ("t1", "sp"), ("t2", "sp"),
                   ("t3", "sp"), ("pq", "sp"), ("pa", "sp"), ("pk", "sp"),
                   ("pf", "sp"), ("out", "sp"),
                   ("w0", "pool"), ("w1", "pool"), ("d0", "pool"), ("d1", "pool")]:
        STR[snm] = (q, mksem("dma_" + snm))
    c_pe = mksem("c_pe")
    c_act = mksem("c_act")
    c_dve = mksem("c_dve")
    c_pool = mksem("c_pool")
    SEMS = dict(pe=c_pe, act=c_act, dve=c_dve, pool=c_pool)

    AR1 = sbuf("AR1", [128, 33280], BF16)
    AR2 = sbuf("AR2", [128, 32768], BF16)
    AR3 = sbuf("AR3", [128, 12288], BF16)
    acc32 = AR2.bitcast(F32)
    ar3f = AR3.bitcast(F32)
    pt_buf = AR3[:, 8192:12288]

    sb_bq = sbuf("bq_c", [128, 16], F32)
    sb_ln1 = sbuf("ln1_c", [128, 8], F32)
    sb_ln2 = sbuf("ln2_c", [128, 8], F32)
    sb_rb = sbuf("routb", [128, 7], F32)
    sel2b = sbuf("sel2b", [2, 256], BF16)
    cn2q = sbuf("cn2q", [2, 2], F32)
    cn2k = sbuf("cn2k", [2, 2], F32)
    cn2t = sbuf("cn2t", [2, 2], F32)
    cnegb = sbuf("cnegb", [2, 1], BF16)
    r_hi = sbuf("r_hi", [1, 512], BF16)
    r_lo = sbuf("r_lo", [1, 512], BF16)
    ones65b = sbuf("ones65b", [1, 65], BF16)
    eps30 = sbuf("eps30", [1, 1], F32)
    lnb64 = sbuf("lnb64", [1, 1], F32)
    eh = sbuf("eh", [128, 2], BF16)

    st_a = sbuf("st_a", [128, 1024], F32)
    st_b = sbuf("st_b", [128, 1024], F32)
    st_c = sbuf("st_c", [128, 1024], F32)
    st_d = sbuf("st_d", [128, 1024], F32)
    XST = [st_a, st_b, st_c, st_d]
    bf_a = sbuf("bf_a", [128, 1024], BF16)
    bf_b = sbuf("bf_b", [128, 1024], BF16)
    att_in = sbuf("att_in", [128, 2048], BF16)
    BST = [bf_a, bf_b]

    wf_a = sbuf("wf_a", [128, 1024], F32)
    wf_b = sbuf("wf_b", [128, 1024], F32)
    WF = [wf_a, wf_b]
    wb_a = sbuf("wb_a", [128, 1024], BF16)
    wb_b = sbuf("wb_b", [128, 1024], BF16)
    WB = [wb_a, wb_b]

    ln_mu = sbuf("ln_mu", [128, 8], F32)
    ln_var = sbuf("ln_var", [128, 8], F32)
    ln_rs = sbuf("ln_rs", [128, 8], F32)
    ln_sq = sbuf("ln_sq", [128, 1024], BF16)
    spacer = sbuf("spacer", [128, 64], F32)
    eps_t = sbuf("eps_t", [128, 1], F32)

    cn_q = sbuf("cn_q", [1, 16], F32)
    cn_k = sbuf("cn_k", [1, 16], F32)
    c_bias = sbuf("c_bias", [128, 16], F32)
    c_hideb = sbuf("c_hideb", [128, 64], F32)
    hide_bc = sbuf("hide_bc", [128, 4], F32)
    r_inv = sbuf("r_inv", [1, 512], F32)
    sc_sm = sbuf("sc_sm", [1, 16], F32)
    s_mask = sbuf("s_mask", [128, 1024], F32)
    trig_c = sbuf("trig_c", [128, 1024], F32)

    wgt_b = sbuf("wgt_b", [128, 8 * 7], F32)
    prob_b = sbuf("prob_b", [128, 8 * 7], F32)
    tp_m = sbuf("tp_m", [128, 8], F32)
    tp_sg = sbuf("tp_sg", [128, 8 * 7], F32)
    tp_p2 = sbuf("tp_p2", [128, 8 * 7], F32)
    tp_den = sbuf("tp_den", [128, 8], F32)
    rt_hi = sbuf("rt_hi", [128, 7 * 8], BF16)
    rt_lo = sbuf("rt_lo", [128, 7 * 8], BF16)
    rt_f = sbuf("rt_f", [128, 7 * 8], F32)
    rt_f2 = sbuf("rt_f2", [128, 7 * 8], F32)

    def emit(eng, fn, inc_self=False):
        b.emit(eng, fn, inc=(SEMS[eng], 1) if inc_self else None)

    def dma(stream, out, in_):
        q, s = STR[stream]
        if s.n > 0:
            b.wait(q, s, s.n)
        b.emit(q, lambda e, o=out, i=in_: e.dma_start(out=o, in_=i), inc=(s, 16))
        return s.n

    tr_rr = [0]

    def dmat(out, in_):
        snm = f"t{tr_rr[0] % 4}"
        tr_rr[0] += 1
        q, s = STR[snm]
        if s.n > 0:
            b.wait(q, s, s.n)
        b.emit(q, lambda e, o=out, i=in_: e.dma_start(out=o, in_=i, transpose=True),
               inc=(s, 16))
        return s

    def wait_trs(eng):
        for i in range(4):
            b.wait(eng, STR[f"t{i}"][1])

    def barrier():
        snap = {k: v.n for k, v in SEMS.items()}
        dsnap = {k: v[1].n for k, v in STR.items()}
        for eng in ("pe", "act", "dve", "pool", "sp"):
            for k, v in snap.items():
                if k != eng:
                    b.wait(eng, SEMS[k], v)
            for k, vn in dsnap.items():
                b.wait(eng, STR[k][1], vn)

    def finish():
        for name, ap in dbg.items():
            dma("out", ap[:, :], DS[name][:, :])
        for nm, ar, cols in [("AR1", AR1, 33280), ("AR2", AR2, 32768),
                             ("AR3", AR3, 8192)]:
            if nm in taps:
                t = nc.dram_tensor("dbg_" + nm, [128, cols], BF16,
                                   kind="ExternalOutput").ap()
                dma("out", t[:, :], ar[:, 0:cols])
        for nm, tn in [("c_bias", c_bias), ("cn2q", cn2q), ("cn2k", cn2k),
                       ("cn2t", cn2t)]:
            if nm in taps:
                shp = list(tn.shape)
                t = nc.dram_tensor("dbg_" + nm, shp, F32,
                                   kind="ExternalOutput").ap()
                dma("out", t[:, :], tn[:, :])
        barrier()
        with nc.Block() as blk:
            @blk.tensor
            def _(t):
                for f in b.q["pe"]:
                    f(t)

            @blk.scalar
            def _(s):
                for f in b.q["act"]:
                    f(s)

            @blk.vector
            def _(v):
                for f in b.q["dve"]:
                    f(v)

            @blk.gpsimd
            def _(g):
                for f in b.q["pool"]:
                    f(g)

            @blk.sync
            def _(s):
                for f in b.q["sp"]:
                    f(s)
        es.close()
        print("instr counts:", {k: len(v) for k, v in b.q.items()})
        return nc

    def dve_space():
        emit("dve", lambda e: e.tensor_copy(spacer[:, :], spacer[:, :]))
        emit("dve", lambda e: e.tensor_copy(spacer[:, :], spacer[:, :]))
        emit("dve", lambda e: e.tensor_copy(spacer[:, :], spacer[:, :]))

    def ln_group(srcs, outs):
        n = len(srcs)
        for r, s in enumerate(srcs):
            emit("dve", lambda e, s=s, r=r: e.reduce_sum(ln_mu[:, r:r + 1], s, axis=AX.X))
        for r in range(n):
            emit("dve", lambda e, r=r: e.tensor_scalar(
                ln_mu[:, r:r + 1], ln_mu[:, r:r + 1], -1.0 / 1024.0, None, op0=OP.mult),
                inc_self=(r == n - 1))
        b.wait("act", c_dve)
        for r, s in enumerate(srcs):
            emit("act", lambda e, s=s, r=r: e.activation(
                ln_sq[:, :], s, AF.Square, bias=ln_mu[:, r:r + 1], scale=1.0,
                accum_out=ln_var[:, r:r + 1]))
        emit("act", lambda e: e.activation(spacer[:, :], spacer[:, :], AF.Copy,
                                           bias=0.0, scale=1.0))
        for r in range(n):
            emit("act", lambda e, r=r: e.activation(
                ln_rs[:, r:r + 1], ln_var[:, r:r + 1], AF.Sqrt,
                bias=eps_t[:, 0:1], scale=1.0 / 1024.0), inc_self=(r == n - 1))
        b.wait("dve", c_act)
        for r in range(n):
            emit("dve", lambda e, r=r: e.reciprocal(ln_rs[:, r:r + 1], ln_rs[:, r:r + 1]))
        dve_space()
        for r, s in enumerate(srcs):
            for jj, dst in enumerate(outs[r]):
                emit("dve", lambda e, s=s, r=r, dst=dst: e.tensor_scalar(
                    dst, s, ln_mu[:, r:r + 1], ln_rs[:, r:r + 1], op0=OP.add, op1=OP.mult),
                    inc_self=(r == n - 1 and jj == len(outs[r]) - 1))

    class WStream:
        def __init__(self):
            self.par = 0
            self.free = [0, 0]

        def load(self, srcs, width):
            par = self.par
            self.par ^= 1
            b.wait("pool", c_pe, self.free[par])
            for col, src in srcs:
                dma("w%d" % par, WF[par][:, col:col + src.shape[-1]], src)
            b.wait("pool", STR["w%d" % par][1])
            emit("pool", lambda e, par=par, width=width: e.tensor_copy(
                WB[par][:, 0:width], WF[par][:, 0:width]), inc_self=True)
            return par, c_pool.n

        def done(self, par):
            self.free[par] = c_pe.n

    ws = WStream()
    _uid = [0]

    def pn_(s):
        _uid[0] += 1
        return f"{s}_{_uid[0]}"

    r_hif = ln_sq.bitcast(F32)[0:1, 0:512]
    r_bc = trig_c[0:65, 0:512]
    emit("dve", lambda e: e.memset(eps_t[:, :], float(EPS)))
    emit("dve", lambda e: e.memset(spacer[:, :], 0.0))
    emit("dve", lambda e: e.memset(c_bias[:, :], 0.0))
    emit("dve", lambda e: e.memset(c_hideb[:, :], 0.0))
    emit("dve", lambda e: e.memset(cn2t[:, :], 0.0))
    emit("dve", lambda e: e.memset(cn2q[:, :], 0.0))
    emit("dve", lambda e: e.memset(cn2k[:, :], 0.0))
    emit("dve", lambda e: e.memset(att_in[:, :], 0.0))
    emit("dve", lambda e: e.memset(eps30[:, :], 1.0e-30))
    emit("dve", lambda e: e.memset(lnb64[:, :], 44.3614195558365))
    emit("dve", lambda e: e.memset(ln_sq[:, :], 0.0))
    emit("dve", lambda e: e.memset(c_bias[:, :], 0.0))
    emit("dve", lambda e: e.memset(c_hideb[:, :], 0.0))
    emit("dve", lambda e: e.memset(cn2t[:, :], 0.0))
    emit("dve", lambda e: e.memset(cn2q[:, :], 0.0))
    emit("dve", lambda e: e.memset(cn2k[:, :], 0.0))
    emit("dve", lambda e: e.memset(att_in[:, :], 0.0))
    emit("dve", lambda e: e.memset(eps30[:, :], 1.0e-30))
    emit("dve", lambda e: e.memset(lnb64[:, :], 44.3614195558365))
    emit("dve", lambda e: e.memset(ln_sq[:, :], 0.0))
    emit("dve", lambda e: e.memset(ones65b[:, :], 1.0))
    emit("dve", lambda e: e.memset(
        AR1.rearrange("p (kt h c) -> p kt h c", h=16, c=65)[:, :, :, 64:65], 1.0),
        inc_self=True)
    for name, dst in [("bqkv_c", sb_bq), ("ln1w_c", sb_ln1), ("ln2w_c", sb_ln2),
                      ("routb_b", sb_rb), ("hide_b", hide_bc), ("ehost", eh),
                      ("sel2", sel2b)]:
        dma("x", dst[:, :], DI[name][:, :])
    b.wait("dve", STR["x"][1])

    def h1T(c):
        return AR2[:, c * 4096:(c + 1) * 4096]

    g1T = h1T

    P0B = [bf_a, bf_b, att_in[:, 0:1024], att_in[:, 1024:2048]]
    xfree = 0
    for g0 in range(0, NT, 4):
        b.wait("sp", c_dve, xfree)
        vx = 0
        for r in range(4):
            tt = g0 + r
            vx = dma("x", XST[r][:, :], DI["x_perm"][tt * 128:(tt + 1) * 128, :])
        b.wait("dve", STR["x"][1], vx)
        b.wait("act", STR["x"][1], vx)
        wait_trs("dve")
        ln_group([XST[r][:, :] for r in range(4)], [[P0B[r][:, :]] for r in range(4)])
        xfree = c_dve.n
        b.wait("sp", c_dve)
        for r in range(4):
            tt = g0 + r
            for c in range(8):
                dmat(h1T(c)[:, tt * 128:(tt + 1) * 128],
                     P0B[r][:, c * 128:(c + 1) * 128])
    barrier()
    if stop_after == "P0":
        return finish()

    with nc.psum_tensor("ps_qk0", [128, 2048], F32) as pqk0, \
         nc.psum_tensor("ps_qk1", [128, 2048], F32) as pqk1:
        PQK = [pqk0, pqk1]
        pfree = [0, 0]
        for oc in range(16):
            par, wv = ws.load([(i * 128, DI["wqkvT"][i * 128:(i + 1) * 128,
                                                     oc * 128:(oc + 1) * 128])
                               for i in range(8)], 1024)
            b.wait("pe", c_pool, wv)
            for th in range(2):
                b.wait("pe", c_dve, pfree[th])
                for i in range(8):
                    for tg in range(4):
                        t0 = th * 2048 + tg * 512
                        emit("pe", lambda e, par=par, i=i, th=th, tg=tg, t0=t0: e.matmul(
                            PQK[th][:, tg * 512:(tg + 1) * 512],
                            WB[par][:, i * 128:(i + 1) * 128],
                            h1T(i)[:, t0:t0 + 512], start=(i == 0), stop=(i == 7)),
                            inc_self=(i == 7 and tg == 3))
                b.wait("dve", c_pe)
                scale = 0.125 if oc < 8 else 1.0
                for hh in range(2):
                    emit("dve", lambda e, th=th, hh=hh, oc=oc, scale=scale: e.tensor_scalar(
                        BST[hh][:, :], PQK[th][:, hh * 1024:(hh + 1) * 1024],
                        sb_bq[:, oc:oc + 1], scale, op0=OP.add, op1=OP.mult),
                        inc_self=(hh == 1))
                pfree[th] = c_dve.n
                b.wait("sp", c_dve)
                park = DS["qt_park"] if oc < 8 else DS["kt_park"]
                prow = (oc % 8) * 128
                for hh in range(2):
                    dma("pq", park[prow:prow + 128,
                               th * 2048 + hh * 1024: th * 2048 + (hh + 1) * 1024],
                        BST[hh][:, :])
            ws.done(par)
    barrier()

    if stop_after == "P1a":
        return finish()
    for i in range(8):
        par, wv = ws.load([(0, DI["wqkvT"][i * 128:(i + 1) * 128, 2048:3072])], 1024)
        b.wait("pool", c_pool, wv)
        emit("pool", lambda e, par=par, i=i: e.tensor_copy(
            AR3[:, i * 1024:(i + 1) * 1024], WB[par][:, :]), inc_self=True)
        ws.free[par] = 0
    v_vb = dma("x", st_c[:, :], DI["bqkvV_b"][:, :])
    b.wait("pe", c_pool)
    b.wait("dve", STR["x"][1], v_vb)

    def vbuf(kt):
        return AR1[:, kt * 1040:(kt + 1) * 1040]

    with nc.psum_tensor("ps_v0", [128, 512], F32) as pv0, \
         nc.psum_tensor("ps_v1", [128, 512], F32) as pv1:
        PVp = [pv0, pv1]
        vfree = [0, 0]
        for tt in range(NT):
            for dvg in range(2):
                b.wait("pe", c_dve, vfree[dvg])
            for i in range(8):
                for dvg in range(2):
                    emit("pe", lambda e, tt=tt, i=i, dvg=dvg: e.matmul(
                        PVp[dvg][:, :], h1T(i)[:, tt * 128:(tt + 1) * 128],
                        AR3[:, i * 1024 + dvg * 512: i * 1024 + (dvg + 1) * 512],
                        start=(i == 0), stop=(i == 7)),
                        inc_self=(i == 7 and dvg == 1))
            b.wait("dve", c_pe)
            for dvg in range(2):
                emit("dve", lambda e, tt=tt, dvg=dvg: e.tensor_tensor(
                    vbuf(tt).rearrange("p (h c) -> p h c", c=65)[:, dvg * 8:(dvg + 1) * 8, 0:64],
                    PVp[dvg][:, :].rearrange("p (h c) -> p h c", c=64),
                    st_c[:, dvg * 512:(dvg + 1) * 512].rearrange("p (h c) -> p h c", c=64),
                    op=OP.add), inc_self=(dvg == 1))
                vfree[dvg] = c_dve.n
    barrier()

    if stop_after == "P1b":
        return finish()
    _cbl = int(_os.environ.get("CB_LIM", "9"))

    def compute_cbias(p, kt_ap, kcols, qt_ap, qcols, with_hide):
        with nc.psum_tensor(pn_("ps_n0"), [2, 512], F32) as pn0, \
             nc.psum_tensor(pn_("ps_n1"), [2, 512], F32) as pn1, \
             nc.psum_tensor(pn_("ps_cb0"), [128, 1], F32) as pcb0, \
             nc.psum_tensor(pn_("ps_cb1"), [128, 1], F32) as pcb1:
            PN = [pn0, pn1]
            PCB = [pcb0, pcb1]
            sq_ring = [ln_sq[:, 0:512], ln_sq[:, 512:1024]]
            pn_free = [0, 0]
            sq_free = [0, 0]
            for src_ap, cols, dst in [(qt_ap, qcols, cn2q), (kt_ap, kcols, cn2k)]:
                ntg = cols // 512
                emit("dve", lambda e, dst=dst: e.memset(dst[:, :], 0.0), inc_self=True)
                for tg in range(ntg):
                    par = tg % 2
                    b.wait("act", c_pe, sq_free[par])
                    emit("act", lambda e, src_ap=src_ap, tg=tg, par=par: e.activation(
                        sq_ring[par], src_ap[:, tg * 512:(tg + 1) * 512], AF.Square,
                        bias=0.0, scale=1.0), inc_self=True)
                    b.wait("pe", c_act)
                    b.wait("pe", c_dve, pn_free[par])
                    emit("pe", lambda e, par=par: e.matmul(
                        PN[par][:, :], eh[:, 0:2], sq_ring[par],
                        start=True, stop=True, skip_group_check=True), inc_self=True)
                    sq_free[par] = c_pe.n
                    b.wait("dve", c_pe)
                    emit("dve", lambda e, par=par: e.tensor_copy(
                        trig_c[0:2, par * 512:(par + 1) * 512], PN[par][:, :]))
                    emit("dve", lambda e, par=par: e.reduce_max(
                        cn2t[:, par:par + 1],
                        trig_c[0:2, par * 512:(par + 1) * 512], axis=AX.X))
                    dve_space()
                    emit("dve", lambda e, dst=dst, par=par: e.tensor_tensor(
                        dst[:, 0:1], dst[:, 0:1], cn2t[:, par:par + 1], op=OP.max),
                        inc_self=True)
                    pn_free[par] = c_dve.n
            if _cbl < 2:
                emit("dve", lambda e, p=p: e.memset(c_bias[:, 2 * p:2 * p + 2], -30.0),
                     inc_self=True)
                return
            emit("dve", lambda e: e.tensor_tensor(
                cn2t[:, 0:1], cn2q[:, 0:1], cn2k[:, 0:1], op=OP.mult), inc_self=True)
            b.wait("act", c_dve)
            emit("act", lambda e: e.activation(
                cn2t[:, 1:2], cn2t[:, 0:1], AF.Sqrt, bias=0.0, scale=1.0),
                inc_self=True)
            b.wait("dve", c_act)
            emit("dve", lambda e: e.tensor_scalar(
                cnegb[:, :], cn2t[:, 1:2], -1.0, 30.0, op0=OP.mult, op1=OP.add),
                inc_self=True)
            if _cbl < 3:
                emit("dve", lambda e, p=p: e.memset(c_bias[:, 2 * p:2 * p + 2], -30.0),
                     inc_self=True)
                return
            b.wait("pe", c_dve)
            for h2 in range(2):
                emit("pe", lambda e, h2=h2: e.matmul(
                    PCB[h2][:, :], sel2b[:, h2 * 128:(h2 + 1) * 128], cnegb[:, :],
                    start=True, stop=True, skip_group_check=True), inc_self=True)
            b.wait("dve", c_pe)
            for h2 in range(2):
                emit("dve", lambda e, p=p, h2=h2: e.tensor_copy(
                    c_bias[:, 2 * p + h2:2 * p + h2 + 1], PCB[h2][:, :]),
                    inc_self=True)
                dve_space()
            if with_hide:
                for sl in range(4):
                    for h2 in range(2):
                        emit("dve", lambda e, sl=sl, p=p, h2=h2: e.tensor_tensor(
                            c_hideb[:, sl * 16 + 2 * p + h2: sl * 16 + 2 * p + h2 + 1],
                            c_bias[:, 2 * p + h2:2 * p + h2 + 1],
                            hide_bc[:, sl:sl + 1], op=OP.add), inc_self=True)

    _atp = _os.environ.get("P2_PART", "full")

    def attention(p, kt_ap, qt_ap, n_qg, qg_col, kranges_fn, hide_fn, dst_fn):
        b.wait("pe", c_dve)
        b.wait("pe", c_act)
        b.wait("act", c_dve)
        with nc.psum_tensor(pn_("ps_s0"), [128, 1024], F32) as ps0, \
             nc.psum_tensor(pn_("ps_s1"), [128, 1024], F32) as ps1, \
             nc.psum_tensor(pn_("ps_pa"), [65, 512], F32) as ppa, \
             nc.psum_tensor(pn_("ps_pb"), [65, 512], F32) as ppb, \
             nc.psum_tensor(pn_("ps_rb"), [65, 512], F32) as prb:
            PS = [ps0, ps1]
            PACC = [ppa, ppb]
            acc_free = [0, 0]
            s_free = [0, 0]
            pt_free = [0, 0]
            for qg in range(n_qg):
                q0 = qg_col(qg)
                hs = hide_fn(qg)
                chunks = []
                for (ktile0, nch, tag) in kranges_fn(qg):
                    for c in range(nch):
                        chunks.append((ktile0 + c, tag))
                ntot = len(chunks)
                for h2 in range(2):
                    b.wait("pe", c_dve, acc_free[h2])
                gi0 = 0
                gidx = 0
                while gi0 < ntot:
                    grp = chunks[gi0:gi0 + 2]
                    ng = len(grp)
                    par = gidx % 2
                    for h2 in range(2):
                        b.wait("pe", c_act, s_free[h2])
                    for h2 in range(2):
                        for gi, (kt, tag) in enumerate(grp):
                            emit("pe", lambda e, h2=h2, kt=kt, gi=gi, q0=q0: e.matmul(
                                PS[h2][:, gi * 512:(gi + 1) * 512],
                                kt_ap[h2 * 64:(h2 + 1) * 64, kt * 128:(kt + 1) * 128],
                                qt_ap[h2 * 64:(h2 + 1) * 64, q0:q0 + 512],
                                start=True, stop=True, tile_position=(h2 * 64, 0),
                                skip_group_check=True),
                                inc_self=(h2 == 1 and gi == ng - 1))
                    v_s = c_pe.n
                    if _atp == "s":
                        gi0 += ng
                        gidx += 1
                        continue
                    b.wait("act", c_pe, v_s)
                    for h2 in range(2):
                        b.wait("act", c_pe, pt_free[par])
                        any_diag = any(tag == "diag" for (_, tag) in grp)
                        if any_diag:
                            b.wait("dve", c_pe, v_s)
                            for gi, (kt, tag) in enumerate(grp):
                                if tag == "diag":
                                    cc = kt % 4
                                    emit("dve", lambda e, h2=h2, gi=gi, cc=cc: e.tensor_tensor(
                                        s_mask[:, gi * 512:(gi + 1) * 512],
                                        PS[h2][:, gi * 512:(gi + 1) * 512],
                                        acc32[:, 9216 + cc * 512: 9216 + (cc + 1) * 512],
                                        op=OP.add), inc_self=True)
                            b.wait("act", c_dve)
                        for gi, (kt, tag) in enumerate(grp):
                            col = 2 * p + h2
                            if tag == "hide" and hs is not None:
                                bias_ap = c_hideb[:, hs * 16 + col: hs * 16 + col + 1]
                            else:
                                bias_ap = c_bias[:, col:col + 1]
                            src = (s_mask[:, gi * 512:(gi + 1) * 512] if tag == "diag"
                                   else PS[h2][:, gi * 512:(gi + 1) * 512])
                            emit("act", lambda e, src=src, h2=h2, par=par, gi=gi,
                                 bias_ap=bias_ap: e.activation(
                                     pt_buf[:, h2 * 2048 + par * 1024 + gi * 512:
                                            h2 * 2048 + par * 1024 + (gi + 1) * 512],
                                     src, AF.Exp, bias=bias_ap, scale=1.0),
                                 inc_self=(gi == ng - 1))
                        s_free[h2] = c_act.n
                    v_e = c_act.n
                    if _atp == "se":
                        gi0 += ng
                        gidx += 1
                        continue
                    b.wait("pe", c_act, v_e)
                    for h2 in range(2):
                        for gi, (kt, tag) in enumerate(grp):
                            emit("pe", lambda e, h2=h2, p=p, kt=kt, par=par, gi=gi,
                                 s=(gi0 + gi == 0), st=(gi0 + gi == ntot - 1): e.matmul(
                                     PACC[h2][:, :],
                                     vbuf(kt)[:, (2 * p + h2) * 65:(2 * p + h2) * 65 + 65],
                                     pt_buf[:, h2 * 2048 + par * 1024 + gi * 512:
                                            h2 * 2048 + par * 1024 + (gi + 1) * 512],
                                     start=s, stop=st, skip_group_check=True),
                                 inc_self=(h2 == 1 and gi == ng - 1))
                    pt_free[par] = c_pe.n
                    gi0 += ng
                    gidx += 1
                v_pv = c_pe.n
                if _atp in ("s", "se", "sep"):
                    for h2 in range(2):
                        acc_free[h2] = 0
                    continue
                for h2 in range(2):
                    b.wait("act", c_pe, v_pv)
                    emit("act", lambda e, h2=h2: e.activation(
                        r_hif[:, :], PACC[h2][64:65, :], AF.Ln,
                        bias=eps30[:, :], scale=1.0))
                    emit("act", lambda e: e.activation(
                        spacer[0:1, 0:64], spacer[0:1, 0:64], AF.Copy,
                        bias=0.0, scale=1.0))
                    emit("act", lambda e: e.activation(
                        r_inv[:, :], r_hif[:, :], AF.Exp, bias=0.0,
                        scale=-1.0), inc_self=True)
                    b.wait("dve", c_act)
                    emit("dve", lambda e: e.tensor_copy(r_hi[:, :], r_inv[:, :]))
                    emit("dve", lambda e: e.tensor_copy(r_hif[:, :], r_hi[:, :]))
                    emit("dve", lambda e: e.tensor_tensor(
                        r_hif[:, :], r_inv[:, :], r_hif[:, :], op=OP.subtract))
                    emit("dve", lambda e: e.tensor_copy(r_lo[:, :], r_hif[:, :]),
                         inc_self=True)
                    b.wait("pe", c_dve)
                    emit("pe", lambda e: e.matmul(prb[:, :], ones65b[:, :], r_hi[:, :],
                                                  start=True, stop=False,
                                                  skip_group_check=True))
                    emit("pe", lambda e: e.matmul(prb[:, :], ones65b[:, :], r_lo[:, :],
                                                  start=False, stop=True,
                                                  skip_group_check=True), inc_self=True)
                    b.wait("act", c_pe)
                    emit("act", lambda e: e.activation(r_bc[:, :], prb[:, :], AF.Copy,
                                                       bias=0.0, scale=1.0), inc_self=True)
                    b.wait("dve", c_act)
                    emit("dve", lambda e, h2=h2, qg=qg: e.tensor_tensor(
                        dst_fn(h2, qg), PACC[h2][0:64, :], r_bc[0:64, :], op=OP.mult),
                        inc_self=True)
                    acc_free[h2] = c_dve.n

    _np2 = int(_os.environ.get("P2_NP", "8"))
    _nqh = int(_os.environ.get("P2_NQH", "2"))
    _nqg = int(_os.environ.get("P2_NQG", "4"))
    _nkc = int(_os.environ.get("P2_NKC", "32"))
    _docb = int(_os.environ.get("P2_CB", "1"))
    _doat = int(_os.environ.get("P2_AT", "1"))
    for p in range(_np2):
        b.wait("sp", c_pe)
        b.wait("sp", c_act)
        vq = dma("x", AR3[:, 0:4096], DS["qt_park"][p * 128:(p + 1) * 128, :])
        vk = dma("x", AR3[:, 4096:8192], DS["kt_park"][p * 128:(p + 1) * 128, :])
        b.wait("act", STR["x"][1], vk)
        b.wait("dve", STR["x"][1], vk)
        b.wait("pe", STR["x"][1], vk)
        if _docb:
            compute_cbias(p, AR3[:, 4096:8192], 4096, AR3[:, 0:4096], 4096,
                          with_hide=False)
        else:
            emit("dve", lambda e, p=p: e.memset(c_bias[:, 2 * p:2 * p + 2], -30.0),
                 inc_self=True)
        if not _doat:
            continue
        for qh in range(_nqh):
            attention(p, AR3[:, 4096:8192], AR3[:, 0:4096], _nqg,
                      qg_col=lambda qg, qh=qh: qh * 2048 + qg * 512,
                      kranges_fn=lambda qg: [(0, _nkc, None)],
                      hide_fn=lambda qg: None,
                      dst_fn=lambda h2, qg: att_in[h2 * 64:(h2 + 1) * 64,
                                                   qg * 512:(qg + 1) * 512])
            b.wait("sp", c_dve)
            dma("pa", DS["att_park"][p * 128:p * 128 + 64, qh * 2048:(qh + 1) * 2048],
                att_in[0:64, :])
            v = dma("pa", DS["att_park"][p * 128 + 64:(p + 1) * 128,
                                         qh * 2048:(qh + 1) * 2048], att_in[64:128, :])
            b.wait("dve", STR["pa"][1], v)
    barrier()

    if stop_after == "P2":
        return finish()
    for i in range(8):
        par, wv = ws.load([(0, DI["woT"][i * 128:(i + 1) * 128, :])], 1024)
        b.wait("pool", c_pool, wv)
        emit("pool", lambda e, par=par, i=i: e.tensor_copy(
            AR3[:, i * 1024:(i + 1) * 1024], WB[par][:, :]), inc_self=True)
        ws.free[par] = 0
    v_bo = dma("x", trig_c[:, 0:1024], DI["bo_b"][:, :])
    b.wait("pe", c_pool)
    b.wait("dve", STR["x"][1], v_bo)

    with nc.psum_tensor("ps_o0", [128, 512], F32) as po0, \
         nc.psum_tensor("ps_o1", [128, 512], F32) as po1:
        PO = [po0, po1]
        ofree = [0, 0]
        att_free = 0
        for tt in range(NT):
            own = tt < 16
            b.wait("sp", c_pe, att_free)
            b.wait("sp", c_dve)
            for fc in range(8):
                va = dma("x", att_in[:, fc * 128:(fc + 1) * 128],
                         DS["att_park"][fc * 128:(fc + 1) * 128, tt * 128:(tt + 1) * 128])
            vx = dma("x", XST[tt % 2][:, :], DI["x_perm"][tt * 128:(tt + 1) * 128, :])
            b.wait("pe", STR["x"][1], va)
            for og in range(2):
                b.wait("pe", c_dve, ofree[og])
            for fc in range(8):
                for og in range(2):
                    emit("pe", lambda e, fc=fc, og=og: e.matmul(
                        PO[og][:, :], att_in[:, fc * 128:(fc + 1) * 128],
                        AR3[:, fc * 1024 + og * 512: fc * 1024 + (og + 1) * 512],
                        start=(fc == 0), stop=(fc == 7)),
                        inc_self=(fc == 7 and og == 1))
            att_free = c_pe.n
            b.wait("dve", c_pe)
            b.wait("dve", STR["pf"][1])
            st_o = XST[2 + tt % 2]
            for og in range(2):
                emit("dve", lambda e, og=og, st_o=st_o: e.tensor_tensor(
                    st_o[:, og * 512:(og + 1) * 512], PO[og][:, :],
                    trig_c[:, og * 512:(og + 1) * 512], op=OP.add),
                    inc_self=(og == 1))
                ofree[og] = c_dve.n
            b.wait("dve", STR["x"][1], vx)
            st_x = XST[tt % 2]
            ln_group([st_o[:, :]], [[st_o[:, :]]])
            emit("dve", lambda e, st_x=st_x, st_o=st_o: e.tensor_tensor(
                st_x[:, :], st_x[:, :], st_o[:, :], op=OP.add), inc_self=True)
            if own:
                b.wait("sp", c_dve)
                dma("pf", DS["x1_park"][tt * 128:(tt + 1) * 128, :], st_x[:, :])
            ln_group([st_x[:, :]], [[st_o[:, :]]])
            if own:
                b.wait("sp", c_dve)
                dma("pf", DS["f_park"][tt * 128:(tt + 1) * 128, :], st_o[:, :])
            wait_trs("dve")
            ln_group([st_o[:, :]], [[BST[tt % 2][:, :]]])
            b.wait("sp", c_dve)
            for c in range(8):
                dmat(g1T(c)[:, tt * 128:(tt + 1) * 128],
                     BST[tt % 2][:, c * 128:(c + 1) * 128])
    barrier()
    for c in range(8):
        emit("act", lambda e, c=c: e.activation(
            g1T(c)[:, :], g1T(c)[:, :], AF.Copy, bias=0.0, scale=sb_ln1[:, c:c + 1]),
            inc_self=True)
    barrier()
    if stop_after == "P3":
        return finish()

    def kvlatT(c):
        return AR3[:, c * 4096:(c + 1) * 4096]

    def qlatT(c):
        return AR3[:, 8192 + c * 2048: 8192 + (c + 1) * 2048]

    with nc.psum_tensor("ps_l", [128, 2048], F32) as pl:
        lfree = [0]
        for wname, dstf, ntcols in [("kvdT", kvlatT, 4096), ("qdT", qlatT, 2048)]:
            for lc in range(2):
                par, wv = ws.load([(i * 128, DI[wname][i * 128:(i + 1) * 128,
                                                       lc * 128:(lc + 1) * 128])
                                   for i in range(8)], 1024)
                b.wait("pe", c_pool, wv)
                for thg in range(ntcols // 2048):
                    b.wait("pe", c_act, lfree[0])
                    for i in range(8):
                        for tg in range(4):
                            t0 = thg * 2048 + tg * 512
                            emit("pe", lambda e, par=par, i=i, tg=tg, t0=t0: e.matmul(
                                pl[:, tg * 512:(tg + 1) * 512],
                                WB[par][:, i * 128:(i + 1) * 128],
                                g1T(i)[:, t0:t0 + 512], start=(i == 0), stop=(i == 7)),
                                inc_self=(i == 7 and tg == 3))
                    b.wait("act", c_pe)
                    for hh in range(2):
                        emit("act", lambda e, dstf=dstf, lc=lc, thg=thg, hh=hh: e.activation(
                            dstf(lc)[:, thg * 2048 + hh * 1024: thg * 2048 + (hh + 1) * 1024],
                            pl[:, hh * 1024:(hh + 1) * 1024], AF.Copy, bias=0.0, scale=1.0),
                            inc_self=(hh == 1))
                    lfree[0] = c_act.n
                ws.done(par)
    barrier()
    if stop_after == "P4a":
        return finish()

    def head_slice_dma(stage_ap, park, oc, tcols0, ncols, row_off):
        for s in range(4):
            h = oc * 4 + s
            dma("pk", park[h * 64 + row_off: h * 64 + row_off + 32,
                           tcols0:tcols0 + ncols],
                stage_ap[s * 32:(s + 1) * 32, :])
        return STR["pk"][1].n

    with nc.psum_tensor("ps_k0", [128, 2048], F32) as pk0:
        kfree = [0]
        pk_last = [0]
        for wname, latf, park, ntcols, row_off, scale in [
                ("kuT", kvlatT, DS["kt2_park"], 4096, 32, 1.0),
                ("quT", qlatT, DS["qt2_park"], 2048, 32, 0.125)]:
            for oc in range(4):
                par, wv = ws.load([(lc * 128, DI[wname][lc * 128:(lc + 1) * 128,
                                                        oc * 128:(oc + 1) * 128])
                                   for lc in range(2)], 256)
                b.wait("pe", c_pool, wv)
                for thg in range(ntcols // 2048):
                    b.wait("pe", c_act, kfree[0])
                    for lc in range(2):
                        for tg in range(4):
                            t0 = thg * 2048 + tg * 512
                            emit("pe", lambda e, par=par, lc=lc, tg=tg, t0=t0, latf=latf:
                                 e.matmul(pk0[:, tg * 512:(tg + 1) * 512],
                                          WB[par][:, lc * 128:(lc + 1) * 128],
                                          latf(lc)[:, t0:t0 + 512],
                                          start=(lc == 0), stop=(lc == 1)),
                                 inc_self=(lc == 1 and tg == 3))
                    b.wait("act", c_pe)
                    b.wait("act", STR["pk"][1], pk_last[0])
                    for hh in range(2):
                        emit("act", lambda e, hh=hh, scale=scale: e.activation(
                            BST[hh][:, :], pk0[:, hh * 1024:(hh + 1) * 1024], AF.Copy,
                            bias=0.0, scale=scale), inc_self=(hh == 1))
                    kfree[0] = c_act.n
                    b.wait("sp", c_act)
                    for hh in range(2):
                        pk_last[0] = head_slice_dma(BST[hh][:, :], park, oc,
                                                    thg * 2048 + hh * 1024, 1024,
                                                    row_off)
                ws.done(par)
    barrier()

    if stop_after == "P4b":
        return finish()
    with nc.psum_tensor("ps_r0", [128, 512], F32) as pr0, \
         nc.psum_tensor("ps_r1", [128, 512], F32) as pr1:
        rfree = [0]
        for wname, wsname, latf, nlc, park, ntcols, ckey, skey in [
                ("rkT", "rksT", g1T, 8, DS["kt2_park"], 4096, "cosk", "sink"),
                ("rqT", "rqsT", qlatT, 2, DS["qt2_park"], 2048, "cosq", "sinq")]:
            for oc in range(4):
                par, wv = ws.load(
                    [(i * 128, DI[wname][i * 128:(i + 1) * 128, oc * 128:(oc + 1) * 128])
                     for i in range(nlc)] +
                    [(512 + i * 128, DI[wsname][i * 128:(i + 1) * 128,
                                                oc * 128:(oc + 1) * 128])
                     for i in range(nlc)] if nlc == 2 else
                    [(i * 128, DI[wname][i * 128:(i + 1) * 128, oc * 128:(oc + 1) * 128])
                     for i in range(8)], 1024 if nlc == 8 else 1024)
                if nlc == 8:
                    par2, wv2 = ws.load(
                        [(i * 128, DI[wsname][i * 128:(i + 1) * 128,
                                              oc * 128:(oc + 1) * 128])
                         for i in range(8)], 1024)
                else:
                    par2, wv2 = par, wv
                b.wait("pe", c_pool, max(wv, wv2))
                for tg in range(ntcols // 512):
                    b.wait("sp", c_dve, rfree[0])
                    vtr = dma("x", trig_c[:, 0:512], DI[ckey][:, tg * 512:(tg + 1) * 512])
                    vtr = dma("x", trig_c[:, 512:1024], DI[skey][:, tg * 512:(tg + 1) * 512])
                    b.wait("pe", c_dve, rfree[0])
                    for i in range(nlc):
                        off = (i * 128) if nlc == 8 else (i * 128)
                        emit("pe", lambda e, par=par, off=off, i=i, tg=tg, latf=latf, nlc=nlc:
                             e.matmul(pr0[:, :], WB[par][:, off:off + 128],
                                      latf(i)[:, tg * 512:(tg + 1) * 512],
                                      start=(i == 0), stop=(i == nlc - 1)))
                    for i in range(nlc):
                        off = (512 + i * 128) if nlc == 2 else (i * 128)
                        emit("pe", lambda e, par2=par2, off=off, i=i, tg=tg, latf=latf,
                             nlc=nlc: e.matmul(pr1[:, :], WB[par2][:, off:off + 128],
                                               latf(i)[:, tg * 512:(tg + 1) * 512],
                                               start=(i == 0), stop=(i == nlc - 1)),
                             inc_self=(i == nlc - 1))
                    b.wait("dve", c_pe)
                    b.wait("dve", STR["x"][1], vtr)
                    emit("dve", lambda e: e.tensor_tensor(
                        s_mask[:, 0:512], pr0[:, :], trig_c[:, 0:512], op=OP.mult))
                    emit("dve", lambda e: e.tensor_tensor(
                        s_mask[:, 512:1024], pr1[:, :], trig_c[:, 512:1024], op=OP.mult),
                        inc_self=True)
                    emit("dve", lambda e: e.tensor_tensor(
                        BST[0][:, 0:512], s_mask[:, 0:512], s_mask[:, 512:1024],
                        op=OP.add), inc_self=True)
                    rfree[0] = c_dve.n
                    b.wait("sp", c_dve)
                    for s in range(4):
                        h = oc * 4 + s
                        scat_v = dma("pk", park[h * 64: h * 64 + 32,
                                                tg * 512:(tg + 1) * 512],
                                     BST[0][s * 32:(s + 1) * 32, 0:512])
                    b.wait("dve", STR["pk"][1], scat_v)
                ws.done(par)
                if nlc == 8:
                    ws.done(par2)
    barrier()
    if stop_after == "P4c":
        return finish()

    vu_par = []
    for lc in range(2):
        par, wv = ws.load([(0, DI["vuT"][lc * 128:(lc + 1) * 128, :])], 1024)
        vu_par.append(par)
    b.wait("pe", c_pool)
    with nc.psum_tensor("ps_v20", [128, 512], F32) as pw0, \
         nc.psum_tensor("ps_v21", [128, 512], F32) as pw1:
        PW = [pw0, pw1]
        vfree = [0, 0]
        for tt in range(NT):
            for dvg in range(2):
                b.wait("pe", c_dve, vfree[dvg])
            for lc in range(2):
                for dvg in range(2):
                    emit("pe", lambda e, lc=lc, tt=tt, dvg=dvg: e.matmul(
                        PW[dvg][:, :], kvlatT(lc)[:, tt * 128:(tt + 1) * 128],
                        WB[vu_par[lc]][:, dvg * 512:(dvg + 1) * 512],
                        start=(lc == 0), stop=(lc == 1)),
                        inc_self=(lc == 1 and dvg == 1))
            b.wait("dve", c_pe)
            for dvg in range(2):
                emit("dve", lambda e, tt=tt, dvg=dvg: e.tensor_copy(
                    vbuf(tt).rearrange("p (h c) -> p h c", c=65)[:, dvg * 8:(dvg + 1) * 8, 0:64],
                    PW[dvg][:, :].rearrange("p (h c) -> p h c", c=64)),
                    inc_self=(dvg == 1))
                vfree[dvg] = c_dve.n
    barrier()

    if stop_after == "P4d":
        return finish()
    v = dma("x", acc32[:, 9216:11264], DI["mask_diag"][:, :])
    b.wait("dve", STR["x"][1], v)

    def attT2(h2, p, qg):
        return AR2[h2 * 64:(h2 + 1) * 64, p * 2048 + qg * 512: p * 2048 + (qg + 1) * 512]

    def mla_kranges(qg):
        r = [(0, 4 * qg, None)] if qg > 0 else []
        r.append((4 * qg, 4, "diag"))
        po = PAD_OTHER[qg]
        if po > 1:
            r.append((16, 4 * (po - 1), None))
        r.append((16 + 4 * (po - 1), 4, "hide"))
        return r

    qt2_sb = AR2[:, 16384:18432]
    for p in range(8):
        b.wait("sp", c_pe)
        b.wait("sp", c_act)
        vk = dma("x", AR3[:, 0:4096], DS["kt2_park"][p * 128:(p + 1) * 128, :])
        vq = dma("x", qt2_sb, DS["qt2_park"][p * 128:(p + 1) * 128, :])
        b.wait("act", STR["x"][1], vq)
        b.wait("dve", STR["x"][1], vq)
        b.wait("pe", STR["x"][1], vq)
        compute_cbias(p, AR3[:, 0:4096], 4096, qt2_sb, 2048, with_hide=True)
        attention(p, AR3[:, 0:4096], qt2_sb, 4,
                  qg_col=lambda qg: qg * 512,
                  kranges_fn=mla_kranges,
                  hide_fn=lambda qg: qg,
                  dst_fn=lambda h2, qg, p=p: attT2(h2, p, qg))
    barrier()
    if stop_after == "P5":
        return finish()

    for i in range(8):
        par, wv = ws.load([(0, DI["owT"][i * 128:(i + 1) * 128, :])], 1024)
        b.wait("pool", c_pool, wv)
        emit("pool", lambda e, par=par, i=i: e.tensor_copy(
            AR3[:, i * 1024:(i + 1) * 1024], WB[par][:, :]), inc_self=True)
        ws.free[par] = 0
    b.wait("pe", c_pool)
    with nc.psum_tensor("ps_m0", [128, 512], F32) as pm0, \
         nc.psum_tensor("ps_m1", [128, 512], F32) as pm1:
        PM = [pm0, pm1]
        mfree = [0, 0]
        for tt in range(16):
            vf = dma("x", XST[tt % 2][:, :], DS["f_park"][tt * 128:(tt + 1) * 128, :])
            for og in range(2):
                b.wait("pe", c_dve, mfree[og])
            for fc in range(8):
                for og in range(2):
                    emit("pe", lambda e, fc=fc, og=og, tt=tt: e.matmul(
                        PM[og][:, :],
                        AR2[:, fc * 2048 + tt * 128: fc * 2048 + tt * 128 + 128],
                        AR3[:, fc * 1024 + og * 512: fc * 1024 + (og + 1) * 512],
                        start=(fc == 0), stop=(fc == 7)),
                        inc_self=(fc == 7 and og == 1))
            b.wait("dve", c_pe)
            b.wait("dve", STR["x"][1], vf)
            for og in range(2):
                emit("dve", lambda e, og=og, tt=tt: e.tensor_tensor(
                    XST[tt % 2][:, og * 512:(og + 1) * 512], PM[og][:, :],
                    XST[tt % 2][:, og * 512:(og + 1) * 512], op=OP.add),
                    inc_self=(og == 1))
                mfree[og] = c_dve.n
            b.wait("sp", c_dve)
            dma("pf", DS["f_park"][tt * 128:(tt + 1) * 128, :], XST[tt % 2][:, :])
    barrier()
    if stop_after == "P6":
        return finish()

    for i in range(8):
        dma("x", rt_f[:, i * 7:(i + 1) * 7], DI["routT"][i * 128:(i + 1) * 128, :])
    b.wait("dve", STR["x"][1])
    emit("dve", lambda e: e.tensor_copy(rt_hi[:, :], rt_f[:, :]), inc_self=True)
    emit("dve", lambda e: e.memset(spacer[:, :], 0.0))
    emit("dve", lambda e: e.memset(c_bias[:, :], 0.0))
    emit("dve", lambda e: e.memset(c_hideb[:, :], 0.0))
    emit("dve", lambda e: e.memset(cn2t[:, :], 0.0))
    emit("dve", lambda e: e.memset(cn2q[:, :], 0.0))
    emit("dve", lambda e: e.memset(cn2k[:, :], 0.0))
    emit("dve", lambda e: e.memset(att_in[:, :], 0.0))
    emit("dve", lambda e: e.memset(eps30[:, :], 1.0e-30))
    emit("dve", lambda e: e.memset(lnb64[:, :], 44.3614195558365))
    emit("dve", lambda e: e.memset(ln_sq[:, :], 0.0))
    emit("dve", lambda e: e.tensor_copy(rt_f2[:, :], rt_hi[:, :]))
    emit("dve", lambda e: e.memset(spacer[:, :], 0.0))
    emit("dve", lambda e: e.memset(c_bias[:, :], 0.0))
    emit("dve", lambda e: e.memset(c_hideb[:, :], 0.0))
    emit("dve", lambda e: e.memset(cn2t[:, :], 0.0))
    emit("dve", lambda e: e.memset(cn2q[:, :], 0.0))
    emit("dve", lambda e: e.memset(cn2k[:, :], 0.0))
    emit("dve", lambda e: e.memset(att_in[:, :], 0.0))
    emit("dve", lambda e: e.memset(eps30[:, :], 1.0e-30))
    emit("dve", lambda e: e.memset(lnb64[:, :], 44.3614195558365))
    emit("dve", lambda e: e.memset(ln_sq[:, :], 0.0))
    emit("dve", lambda e: e.tensor_tensor(rt_f2[:, :], rt_f[:, :], rt_f2[:, :],
                                          op=OP.subtract))
    emit("dve", lambda e: e.memset(spacer[:, :], 0.0))
    emit("dve", lambda e: e.memset(c_bias[:, :], 0.0))
    emit("dve", lambda e: e.memset(c_hideb[:, :], 0.0))
    emit("dve", lambda e: e.memset(cn2t[:, :], 0.0))
    emit("dve", lambda e: e.memset(cn2q[:, :], 0.0))
    emit("dve", lambda e: e.memset(cn2k[:, :], 0.0))
    emit("dve", lambda e: e.memset(att_in[:, :], 0.0))
    emit("dve", lambda e: e.memset(eps30[:, :], 1.0e-30))
    emit("dve", lambda e: e.memset(lnb64[:, :], 44.3614195558365))
    emit("dve", lambda e: e.memset(ln_sq[:, :], 0.0))
    emit("dve", lambda e: e.tensor_copy(rt_lo[:, :], rt_f2[:, :]), inc_self=True)

    def xt_c(c):
        return AR3[:, c * 1024:(c + 1) * 1024]

    def xlo_c(c):
        return AR1[:, 16384 + c * 1024: 16384 + (c + 1) * 1024]

    def h_c(c):
        return AR1[:, c * 1024:(c + 1) * 1024]

    def dtile(i2):
        return AR1[:, 16384 + i2 * 1024: 16384 + (i2 + 1) * 1024]

    P0B2 = [bf_a, bf_b, att_in[:, 0:1024], att_in[:, 1024:2048]]
    DST32 = [st_c, st_d]

    for hf in range(2):
        for g0 in range(0, 8, 4):
            b.wait("sp", c_dve)
            vx = 0
            for r in range(4):
                tl = g0 + r
                row = (hf * 8 + tl) * 128
                vx = dma("x", XST[r][:, :], DS["f_park"][row:row + 128, :])
            b.wait("dve", STR["x"][1], vx)
            b.wait("act", STR["x"][1], vx)
            ln_group([XST[r][:, :] for r in range(4)],
                     [[P0B2[r][:, :], XST[r][:, :]] for r in range(4)])
            for r in range(4):
                tl = g0 + r
                emit("dve", lambda e, r=r: e.tensor_copy(s_mask[:, :], P0B2[r][:, :]))
                emit("dve", lambda e, r=r: e.tensor_tensor(
                    ln_sq[:, :], XST[r][:, :], s_mask[:, :], op=OP.subtract),
                    inc_self=True)
                b.wait("sp", c_dve)
                for c in range(8):
                    dmat(xt_c(c)[:, tl * 128:(tl + 1) * 128],
                         P0B2[r][:, c * 128:(c + 1) * 128])
                for c in range(8):
                    dmat(xlo_c(c)[:, tl * 128:(tl + 1) * 128],
                         ln_sq[:, c * 128:(c + 1) * 128])
                wait_trs("dve")
        barrier()
        for c in range(8):
            emit("act", lambda e, c=c: e.activation(
                xt_c(c)[:, :], xt_c(c)[:, :], AF.Copy, bias=0.0,
                scale=sb_ln2[:, c:c + 1]))
            emit("act", lambda e, c=c: e.activation(
                xlo_c(c)[:, :], xlo_c(c)[:, :], AF.Copy, bias=0.0,
                scale=sb_ln2[:, c:c + 1]), inc_self=(c == 7))
        barrier()

        with nc.psum_tensor(pn_("ps_r7"), [128, 7], F32) as pr7:
            pr_free = 0
            for tl in range(8):
                b.wait("pe", c_dve, pr_free)
                seq = []
                for c in range(8):
                    seq.append((xt_c(c), None))
                    seq.append((xt_c(c), rt_hi, c))
                    seq.append((xt_c(c), rt_lo, c))
                for c in range(8):
                    seq.append((xlo_c(c), None))
                    seq.append((xlo_c(c), rt_hi, c))
                nmm = sum(1 for s in seq if len(s) == 3)
                k = 0
                for item in seq:
                    if len(item) == 2:
                        xc = item[0]
                    else:
                        xc, rt, c = item
                        k += 1
                        emit("pe", lambda e, xc=xc, tl=tl, rt=rt, c=c,
                             s=(k == 1), t=(k == nmm): e.matmul(
                                 pr7[:, :], xc[:, tl * 128:(tl + 1) * 128],
                                 rt[:, c * 7:(c + 1) * 7], start=s, stop=t,
                                 skip_group_check=True),
                             inc_self=(k == nmm))
                b.wait("dve", c_pe)
                emit("dve", lambda e, tl=tl: e.tensor_tensor(
                    prob_b[:, tl * 7:(tl + 1) * 7], pr7[:, :], sb_rb[:, :],
                    op=OP.add), inc_self=True)
                pr_free = c_dve.n
        b.wait("act", c_dve)
        emit("act", lambda e: e.activation(prob_b[:, :], prob_b[:, :], AF.Sigmoid,
                                           bias=0.0, scale=1.0), inc_self=True)
        b.wait("dve", c_act)
        for tl in range(8):
            emit("dve", lambda e, tl=tl: e.reduce_max(
                tp_m[:, tl:tl + 1], prob_b[:, tl * 7:(tl + 1) * 7], axis=AX.X),
                inc_self=(tl == 7))
        b.wait("act", c_dve)
        for tl in range(8):
            emit("act", lambda e, tl=tl: e.activation(
                tp_sg[:, tl * 7:(tl + 1) * 7], prob_b[:, tl * 7:(tl + 1) * 7],
                AF.Sign, bias=tp_m[:, tl:tl + 1], scale=-1.0), inc_self=(tl == 7))
        b.wait("dve", c_act)
        for tl in range(8):
            emit("dve", lambda e, tl=tl: e.tensor_scalar(
                tp_sg[:, tl * 7:(tl + 1) * 7], tp_sg[:, tl * 7:(tl + 1) * 7],
                -1.0, 1.0, op0=OP.mult, op1=OP.add))
        for tl in range(8):
            emit("dve", lambda e, tl=tl: e.scalar_tensor_tensor(
                tp_p2[:, tl * 7:(tl + 1) * 7], tp_sg[:, tl * 7:(tl + 1) * 7],
                -1.0e9, prob_b[:, tl * 7:(tl + 1) * 7], op0=OP.mult, op1=OP.add))
        for tl in range(8):
            emit("dve", lambda e, tl=tl: e.reduce_max(
                tp_m[:, tl:tl + 1], tp_p2[:, tl * 7:(tl + 1) * 7], axis=AX.X),
                inc_self=(tl == 7))
        b.wait("act", c_dve)
        for tl in range(8):
            emit("act", lambda e, tl=tl: e.activation(
                rt_f2[:, tl * 7:(tl + 1) * 7], tp_p2[:, tl * 7:(tl + 1) * 7],
                AF.Sign, bias=tp_m[:, tl:tl + 1], scale=-1.0), inc_self=(tl == 7))
        b.wait("dve", c_act)
        for tl in range(8):
            emit("dve", lambda e, tl=tl: e.tensor_scalar(
                rt_f2[:, tl * 7:(tl + 1) * 7], rt_f2[:, tl * 7:(tl + 1) * 7],
                -1.0, 1.0, op0=OP.mult, op1=OP.add))
        for tl in range(8):
            emit("dve", lambda e, tl=tl: e.tensor_tensor(
                tp_sg[:, tl * 7:(tl + 1) * 7], tp_sg[:, tl * 7:(tl + 1) * 7],
                rt_f2[:, tl * 7:(tl + 1) * 7], op=OP.add))
        for tl in range(8):
            emit("dve", lambda e, tl=tl: e.tensor_tensor(
                tp_p2[:, tl * 7:(tl + 1) * 7], prob_b[:, tl * 7:(tl + 1) * 7],
                tp_sg[:, tl * 7:(tl + 1) * 7], op=OP.mult))
        for tl in range(8):
            emit("dve", lambda e, tl=tl: e.reduce_sum(
                tp_den[:, tl:tl + 1], tp_p2[:, tl * 7:(tl + 1) * 7], axis=AX.X))
        dve_space()
        for tl in range(8):
            emit("dve", lambda e, tl=tl: e.reciprocal(
                tp_den[:, tl:tl + 1], tp_den[:, tl:tl + 1]))
        dve_space()
        for tl in range(8):
            emit("dve", lambda e, tl=tl: e.tensor_scalar(
                wgt_b[:, tl * 7:(tl + 1) * 7], tp_p2[:, tl * 7:(tl + 1) * 7],
                tp_den[:, tl:tl + 1], None, op0=OP.mult), inc_self=(tl == 7))

        with nc.psum_tensor(pn_("ps_g"), [128, 1024], F32) as pg, \
             nc.psum_tensor(pn_("ps_u"), [128, 1024], F32) as pu, \
             nc.psum_tensor(pn_("ps_d0"), [128, 512], F32) as pd0, \
             nc.psum_tensor(pn_("ps_d1"), [128, 512], F32) as pd1:
            PD = [pd0, pd1]
            b.wait("pe", c_dve)
            b.wait("pe", c_act)
            down_done = c_pe.n
            g_free = 0
            u_free = 0
            h_free = [0] * 16
            acc_free = 0
            for e_i in range(8):
                b.wait("pool", c_pe, down_done)
                dvals = {}
                dvals[0] = dma("d0", DST32[0][:, :], DI["downT"][e_i, 0:128, :])
                for i2 in range(16):
                    tp = i2 % 2
                    if i2 < 15:
                        dvals[i2 + 1] = dma("d%d" % (1 - tp), DST32[1 - tp][:, :],
                                            DI["downT"][e_i,
                                                        (i2 + 1) * 128:(i2 + 2) * 128, :])
                    b.wait("pool", STR["d%d" % tp][1], dvals[i2])
                    emit("pool", lambda e, i2=i2, tp=tp: e.tensor_copy(
                        dtile(i2)[:, :], DST32[tp][:, :]), inc_self=True)
                d_ready = c_pool.n
                for oc in range(16):
                    parg, wvg = ws.load([(i * 128,
                                          DI["gateT"][e_i, i * 128:(i + 1) * 128,
                                                      oc * 128:(oc + 1) * 128])
                                         for i in range(8)], 1024)
                    b.wait("pe", c_pool, wvg)
                    b.wait("pe", c_act, g_free)
                    for i in range(8):
                        for tg in range(2):
                            emit("pe", lambda e, parg=parg, i=i, tg=tg, hf=hf: e.matmul(
                                pg[:, tg * 512:(tg + 1) * 512],
                                WB[parg][:, i * 128:(i + 1) * 128],
                                xt_c(i)[:, tg * 512:(tg + 1) * 512],
                                start=(i == 0), stop=(i == 7)),
                                inc_self=(i == 7 and tg == 1))
                    ws.done(parg)
                    v_g = c_pe.n
                    paru, wvu = ws.load([(i * 128,
                                          DI["upT"][e_i, i * 128:(i + 1) * 128,
                                                    oc * 128:(oc + 1) * 128])
                                         for i in range(8)], 1024)
                    b.wait("pe", c_pool, wvu)
                    b.wait("pe", c_dve, u_free)
                    for i in range(8):
                        for tg in range(2):
                            emit("pe", lambda e, paru=paru, i=i, tg=tg: e.matmul(
                                pu[:, tg * 512:(tg + 1) * 512],
                                WB[paru][:, i * 128:(i + 1) * 128],
                                xt_c(i)[:, tg * 512:(tg + 1) * 512],
                                start=(i == 0), stop=(i == 7)),
                                inc_self=(i == 7 and tg == 1))
                    ws.done(paru)
                    v_u = c_pe.n
                    b.wait("act", c_pe, v_g)
                    emit("act", lambda e: e.activation(ln_sq[:, :], pg[:, :], AF.Sigmoid,
                                                       bias=0.0, scale=1.0),
                         inc_self=True)
                    b.wait("dve", c_act)
                    emit("dve", lambda e: e.tensor_tensor(
                        st_a[:, :], pg[:, :], ln_sq[:, :], op=OP.mult), inc_self=True)
                    g_free = c_dve.n
                    b.wait("dve", c_pe, v_u)
                    b.wait("dve", c_pe, h_free[oc])
                    emit("dve", lambda e, oc=oc: e.tensor_tensor(
                        h_c(oc)[:, :], st_a[:, :], pu[:, :], op=OP.mult), inc_self=True)
                    u_free = c_dve.n
                b.wait("pe", c_pool, d_ready)
                b.wait("pe", c_dve, u_free)
                for tl in range(8):
                    b.wait("pe", c_dve, acc_free)
                    for i2 in range(16):
                        for og in range(2):
                            emit("pe", lambda e, i2=i2, tl=tl, og=og: e.matmul(
                                PD[og][:, :], h_c(i2)[:, tl * 128:(tl + 1) * 128],
                                dtile(i2)[:, og * 512:(og + 1) * 512],
                                start=(i2 == 0), stop=(i2 == 15)),
                                inc_self=(i2 == 15 and og == 1))
                    b.wait("dve", c_pe)
                    for og in range(2):
                        if e_i == 0:
                            emit("dve", lambda e, tl=tl, og=og: e.tensor_copy(
                                acc32[:, tl * 1024 + og * 512: tl * 1024 + (og + 1) * 512],
                                PD[og][:, :]), inc_self=(og == 1))
                        else:
                            emit("dve", lambda e, tl=tl, og=og, e_i=e_i:
                                 e.scalar_tensor_tensor(
                                     acc32[:, tl * 1024 + og * 512:
                                           tl * 1024 + (og + 1) * 512],
                                     PD[og][:, :], wgt_b[:, tl * 7 + e_i - 1:
                                                         tl * 7 + e_i], 
                                     acc32[:, tl * 1024 + og * 512:
                                           tl * 1024 + (og + 1) * 512],
                                     op0=OP.mult, op1=OP.add), inc_self=(og == 1))
                    acc_free = c_dve.n
                for oc in range(16):
                    h_free[oc] = c_pe.n
                down_done = c_pe.n

        barrier()
        for g0 in range(0, 8, 4):
            b.wait("sp", c_dve)
            vx = 0
            for r in range(4):
                tl = g0 + r
                row = (hf * 8 + tl) * 128
                vx = dma("x", XST[r][:, :], DS["f_park"][row:row + 128, :])
            b.wait("dve", STR["x"][1], vx)
            for r in range(4):
                tl = g0 + r
                emit("dve", lambda e, r=r, tl=tl: e.tensor_tensor(
                    XST[r][:, :], XST[r][:, :], acc32[:, tl * 1024:(tl + 1) * 1024],
                    op=OP.add), inc_self=(r == 3))
            b.wait("act", c_dve)
            ln_group([XST[r][:, :] for r in range(4)],
                     [[acc32[:, (g0 + r) * 1024:(g0 + r + 1) * 1024]] for r in range(4)])
            b.wait("sp", c_dve)
            vx = 0
            for r in range(4):
                tl = g0 + r
                row = (hf * 8 + tl) * 128
                vx = dma("x", XST[r][:, :], DS["x1_park"][row:row + 128, :])
            b.wait("dve", STR["x"][1], vx)
            for r in range(4):
                tl = g0 + r
                emit("dve", lambda e, r=r, tl=tl: e.tensor_tensor(
                    XST[r][:, :], XST[r][:, :], acc32[:, tl * 1024:(tl + 1) * 1024],
                    op=OP.add), inc_self=True)
                b.wait("sp", c_dve)
                row = (hf * 8 + tl) * 128
                dma("out", out_d[row:row + 128, :], XST[r][:, :])
        barrier()

    return finish()


def kernel(**inputs):
    nc = build_program()
    in_maps = []
    row_maps = []
    for core in range(8):
        bb, g = core // 2, core % 2
        d, rows = _prep_core_inputs(inputs, bb, g)
        in_maps.append(d)
        row_maps.append(rows[:2048])
    res = run_bass_kernel_spmd(nc, in_maps, core_ids=list(range(8)))
    out = np.empty((4, TSEQ, DM), dtype=np.float32)
    for core in range(8):
        bb = core // 2
        out[bb, row_maps[core]] = np.asarray(res.results[core]["out"], dtype=np.float32)
    return out


# version 03
# speedup vs baseline: 1.0002x; 1.0002x over previous
import contextlib
import os as _os
import numpy as np
import ml_dtypes
import concourse.bass as bass
import concourse.mybir as mybir
from concourse.bass_utils import run_bass_kernel_spmd
from concourse.alu_op_type import AluOpType as OP

F32 = mybir.dt.float32
BF16 = mybir.dt.bfloat16
AF = mybir.ActivationFunctionType
AX = mybir.AxisListType
bfdt = ml_dtypes.bfloat16

TSEQ, DM = 4096, 1024
NBLK, BLK = 8, 512
NT = 32
OWN_BLOCKS = [[0, 3, 4, 7], [1, 2, 5, 6]]
PAD_OTHER = [1, 2, 3, 4]
NEG = -1.0e30
EPS = 1e-5


class Ctr:
    def __init__(self, h):
        self.h = h
        self.n = 0


class Bld:
    ENGS = ("pe", "act", "dve", "pool", "sp")

    def __init__(self):
        self.q = {e: [] for e in self.ENGS}

    def emit(self, eng, fn, inc=None):
        if inc is not None:
            incs = inc if isinstance(inc, list) else [inc]
            vals = []
            for s, k in incs:
                s.n += k
                vals.append((s.h, k))

            def run(e, fn=fn, vals=vals):
                ins = fn(e)
                for h, k in vals:
                    ins = ins.then_inc(h, k)
            self.q[eng].append(run)
        else:
            self.q[eng].append(fn)

    def wait(self, eng, sem, val=None):
        v = sem.n if val is None else val
        if v <= 0:
            return
        self.q[eng].append(lambda e, h=sem.h, v=v: e.wait_ge(h, v))


def _prep_core_inputs(inp, bb, g):
    f32 = np.float32
    own = OWN_BLOCKS[g]
    other = [blk for blk in range(NBLK) if blk not in own]
    rows = np.concatenate([np.arange(blk * BLK, (blk + 1) * BLK) for blk in own + other])

    d = {"x_perm": np.ascontiguousarray(np.asarray(inp["L0_attn_x"][bb], dtype=f32)[rows])}

    Wqkv = np.asarray(inp["L0_attn_Wqkv"], dtype=f32)
    bqkv = np.asarray(inp["L0_attn_bqkv"], dtype=f32)
    idx = []
    for part in range(3):
        for h in range(16):
            idx.extend(range(h * 192 + part * 64, h * 192 + part * 64 + 64))
    idx = np.array(idx)
    d["wqkvT"] = np.ascontiguousarray(Wqkv[idx].T)
    bq = bqkv[idx]
    d["bqkv_c"] = np.ascontiguousarray(bq[:2048].reshape(16, 128).T)
    d["bqkvV_b"] = np.ascontiguousarray(np.tile(bq[2048:], (128, 1)))
    d["woT"] = np.ascontiguousarray(np.asarray(inp["L0_attn_Wo"], f32).T)
    d["bo_b"] = np.ascontiguousarray(np.tile(np.asarray(inp["L0_attn_bo"], f32), (128, 1)))

    d["ln1w_c"] = np.ascontiguousarray(np.asarray(inp["L0_ffn_ln1_w"], f32).reshape(8, 128).T)
    d["ln2w_c"] = np.ascontiguousarray(np.asarray(inp["L0_ffn_ln2_w"], f32).reshape(8, 128).T)
    d["kvdT"] = np.ascontiguousarray(np.asarray(inp["L0_ffn_kv_d"], f32).T)
    d["qdT"] = np.ascontiguousarray(np.asarray(inp["L0_ffn_q_d"], f32).T)
    k_u = np.asarray(inp["L0_ffn_k_u"], f32)
    q_u = np.asarray(inp["L0_ffn_q_u"], f32)
    nope = np.concatenate([np.arange(h * 64 + 32, h * 64 + 64) for h in range(16)])
    d["kuT"] = np.ascontiguousarray(k_u[nope].T)
    d["quT"] = np.ascontiguousarray(q_u[nope].T)
    d["vuT"] = np.ascontiguousarray(np.asarray(inp["L0_ffn_v_u"], f32).T)

    rq = np.asarray(inp["L0_ffn_rope_q_w"], f32)
    rk = np.asarray(inp["L0_ffn_rope_k_w"], f32)
    swap = np.array(sum([list(range(h * 32 + 16, h * 32 + 32)) +
                         list(range(h * 32, h * 32 + 16)) for h in range(16)], []),
                    dtype=np.int64)
    d["rqT"] = np.ascontiguousarray(rq.T)
    d["rqsT"] = np.ascontiguousarray(rq[swap].T)
    d["rkT"] = np.ascontiguousarray(rk.T)
    d["rksT"] = np.ascontiguousarray(rk[swap].T)
    d["owT"] = np.ascontiguousarray(np.asarray(inp["L0_ffn_o_w"], f32).T)
    d["routT"] = np.ascontiguousarray(np.asarray(inp["L0_ffn_router_w"], f32).T)
    d["routb_b"] = np.ascontiguousarray(np.tile(np.asarray(inp["L0_ffn_routing_bias"], f32),
                                                (128, 1)))
    gate = np.concatenate([np.asarray(inp["L0_ffn_shared_gate"], f32),
                           np.asarray(inp["L0_ffn_routed_gate"], f32)], axis=0)
    up = np.concatenate([np.asarray(inp["L0_ffn_shared_up"], f32),
                         np.asarray(inp["L0_ffn_routed_up"], f32)], axis=0)
    down = np.concatenate([np.asarray(inp["L0_ffn_shared_down"], f32),
                           np.asarray(inp["L0_ffn_routed_down"], f32)], axis=0)
    d["gateT"] = np.ascontiguousarray(np.transpose(gate, (0, 2, 1)))
    d["upT"] = np.ascontiguousarray(np.transpose(up, (0, 2, 1)))
    d["downT"] = np.ascontiguousarray(np.transpose(down, (0, 2, 1)))

    inv_freq = (1.0 / (10000.0 ** (np.arange(0, 32, 2, dtype=f32) / f32(32.0)))).astype(f32)
    tpos = np.arange(TSEQ, dtype=f32)
    emb = np.concatenate([tpos[:, None] * inv_freq[None, :]] * 2, axis=-1)
    cos = np.cos(emb).astype(f32)
    sin = np.sin(emb).astype(f32)
    sgn = np.where(np.arange(32) < 16, -1.0, 1.0).astype(f32)
    cosk = np.tile(cos[rows].T, (4, 1))
    sink = np.tile((sin[rows] * sgn).T, (4, 1))
    d["cosk"] = np.ascontiguousarray(cosk)
    d["sink"] = np.ascontiguousarray(sink)
    d["cosq"] = np.ascontiguousarray(cosk[:, :2048] * f32(0.125))
    d["sinq"] = np.ascontiguousarray(sink[:, :2048] * f32(0.125))

    j = np.arange(BLK)
    diag = np.empty((128, 4 * BLK), dtype=f32)
    for cc in range(4):
        kk = cc * 128 + np.arange(128)
        diag[:, cc * BLK:(cc + 1) * BLK] = np.where(kk[:, None] <= j[None, :], 0.0, NEG)
    d["mask_diag"] = np.ascontiguousarray(diag)
    hide = np.zeros((1, 4), dtype=f32)
    for i, blk in enumerate(own):
        if sum(1 for ob in other if ob < blk) < PAD_OTHER[i]:
            hide[0, i] = NEG
    d["hide_b"] = np.ascontiguousarray(np.tile(hide, (128, 1)))
    eh = np.zeros((128, 2), dtype=f32)
    eh[0:64, 0] = 1.0
    eh[64:128, 1] = 1.0
    import ml_dtypes as _mld
    d["ehost"] = eh.astype(_mld.bfloat16)
    sel2 = np.zeros((2, 256), dtype=f32)
    sel2[0, 0:128] = 1.0
    sel2[1, 128:256] = 1.0
    d["sel2"] = sel2.astype(_mld.bfloat16)
    return d, rows


def build_program(taps=(), stop_after=None):
    nc = bass.Bass("TRN2", target_bir_lowering=False, debug=False,
                   detect_race_conditions=False)
    b = Bld()
    es = contextlib.ExitStack()

    DI = {}
    for name, shape in [
        ("x_perm", [TSEQ, DM]), ("wqkvT", [DM, 3072]), ("bqkv_c", [128, 16]),
        ("bqkvV_b", [128, DM]), ("woT", [DM, DM]), ("bo_b", [128, DM]),
        ("ln1w_c", [128, 8]), ("ln2w_c", [128, 8]),
        ("kvdT", [DM, 256]), ("qdT", [DM, 256]), ("kuT", [256, 512]),
        ("quT", [256, 512]), ("vuT", [256, DM]),
        ("rqT", [256, 512]), ("rqsT", [256, 512]), ("rkT", [DM, 512]), ("rksT", [DM, 512]),
        ("owT", [DM, DM]), ("routT", [DM, 7]), ("routb_b", [128, 7]),
        ("gateT", [8, DM, 2048]), ("upT", [8, DM, 2048]), ("downT", [8, 2048, DM]),
        ("cosk", [128, TSEQ]), ("sink", [128, TSEQ]),
        ("cosq", [128, 2048]), ("sinq", [128, 2048]),
        ("mask_diag", [128, 4 * BLK]), ("hide_b", [128, 4]),
    ]:
        DI[name] = nc.dram_tensor(name, shape, F32, kind="ExternalInput").ap()
    DI["ehost"] = nc.dram_tensor("ehost", [128, 2], BF16, kind="ExternalInput").ap()
    DI["sel2"] = nc.dram_tensor("sel2", [2, 256], BF16, kind="ExternalInput").ap()
    out_d = nc.dram_tensor("out", [2048, DM], F32, kind="ExternalOutput").ap()

    DS = {}
    park_shapes = {
        "qt_park": ([DM, TSEQ], BF16), "kt_park": ([DM, TSEQ], BF16),
        "att_park": ([DM, TSEQ], BF16), "qt2_park": ([DM, 2048], BF16),
        "kt2_park": ([DM, TSEQ], BF16), "x1_park": ([2048, DM], F32),
        "f_park": ([2048, DM], F32),
    }
    for name, (shape, dt) in park_shapes.items():
        DS[name] = nc.dram_tensor(name, shape, dt, kind="Internal").ap()
    dbg = {}
    for name in taps:
        if name not in park_shapes:
            continue
        shp, dt = park_shapes[name]
        dbg[name] = nc.dram_tensor("dbg_" + name, shp, dt, kind="ExternalOutput").ap()

    def sbuf(name, shape, dt):
        return es.enter_context(nc.sbuf_tensor(name, list(shape), dt))

    def mksem(name):
        return Ctr(es.enter_context(nc.semaphore(name)))

    STR = {}
    for snm, q in [("x", "sp"), ("t0", "sp"), ("t1", "sp"), ("t2", "sp"),
                   ("t3", "sp"), ("pq", "sp"), ("pa", "sp"), ("pk", "sp"),
                   ("pf", "sp"), ("out", "sp"),
                   ("w0", "pool"), ("w1", "pool"), ("d0", "pool"), ("d1", "pool")]:
        STR[snm] = (q, mksem("dma_" + snm))
    c_pe = mksem("c_pe")
    c_act = mksem("c_act")
    c_dve = mksem("c_dve")
    c_pool = mksem("c_pool")
    SEMS = dict(pe=c_pe, act=c_act, dve=c_dve, pool=c_pool)

    AR1 = sbuf("AR1", [128, 33280], BF16)
    AR2 = sbuf("AR2", [128, 32768], BF16)
    AR3 = sbuf("AR3", [128, 12288], BF16)
    acc32 = AR2.bitcast(F32)
    ar3f = AR3.bitcast(F32)
    pt_buf = AR3[:, 8192:12288]

    sb_bq = sbuf("bq_c", [128, 16], F32)
    sb_ln1 = sbuf("ln1_c", [128, 8], F32)
    sb_ln2 = sbuf("ln2_c", [128, 8], F32)
    sb_rb = sbuf("routb", [128, 7], F32)
    sel2b = sbuf("sel2b", [2, 256], BF16)
    cn2q = sbuf("cn2q", [2, 2], F32)
    cn2k = sbuf("cn2k", [2, 2], F32)
    cn2t = sbuf("cn2t", [2, 2], F32)
    cnegb = sbuf("cnegb", [2, 1], BF16)
    r_hi = sbuf("r_hi", [1, 512], BF16)
    r_lo = sbuf("r_lo", [1, 512], BF16)
    ones65b = sbuf("ones65b", [1, 65], BF16)
    eps30 = sbuf("eps30", [1, 1], F32)
    lnb64 = sbuf("lnb64", [1, 1], F32)
    eh = sbuf("eh", [128, 2], BF16)

    st_a = sbuf("st_a", [128, 1024], F32)
    st_b = sbuf("st_b", [128, 1024], F32)
    st_c = sbuf("st_c", [128, 1024], F32)
    st_d = sbuf("st_d", [128, 1024], F32)
    XST = [st_a, st_b, st_c, st_d]
    bf_a = sbuf("bf_a", [128, 1024], BF16)
    bf_b = sbuf("bf_b", [128, 1024], BF16)
    att_in = sbuf("att_in", [128, 2048], BF16)
    BST = [bf_a, bf_b]

    wf_a = sbuf("wf_a", [128, 1024], F32)
    wf_b = sbuf("wf_b", [128, 1024], F32)
    WF = [wf_a, wf_b]
    wb_a = sbuf("wb_a", [128, 1024], BF16)
    wb_b = sbuf("wb_b", [128, 1024], BF16)
    WB = [wb_a, wb_b]

    ln_mu = sbuf("ln_mu", [128, 8], F32)
    ln_var = sbuf("ln_var", [128, 8], F32)
    ln_rs = sbuf("ln_rs", [128, 8], F32)
    ln_sq = sbuf("ln_sq", [128, 1024], BF16)
    spacer = sbuf("spacer", [128, 64], F32)
    eps_t = sbuf("eps_t", [128, 1], F32)

    cn_q = sbuf("cn_q", [1, 16], F32)
    cn_k = sbuf("cn_k", [1, 16], F32)
    c_bias = sbuf("c_bias", [128, 16], F32)
    c_hideb = sbuf("c_hideb", [128, 64], F32)
    hide_bc = sbuf("hide_bc", [128, 4], F32)
    r_inv = sbuf("r_inv", [1, 512], F32)
    sc_sm = sbuf("sc_sm", [1, 16], F32)
    s_mask = sbuf("s_mask", [128, 1024], F32)
    trig_c = sbuf("trig_c", [128, 1024], F32)

    wgt_b = sbuf("wgt_b", [128, 8 * 7], F32)
    prob_b = sbuf("prob_b", [128, 8 * 7], F32)
    tp_m = sbuf("tp_m", [128, 8], F32)
    tp_sg = sbuf("tp_sg", [128, 8 * 7], F32)
    tp_p2 = sbuf("tp_p2", [128, 8 * 7], F32)
    tp_den = sbuf("tp_den", [128, 8], F32)
    rt_hi = sbuf("rt_hi", [128, 7 * 8], BF16)
    rt_lo = sbuf("rt_lo", [128, 7 * 8], BF16)
    rt_f = sbuf("rt_f", [128, 7 * 8], F32)
    rt_f2 = sbuf("rt_f2", [128, 7 * 8], F32)

    def emit(eng, fn, inc_self=False):
        b.emit(eng, fn, inc=(SEMS[eng], 1) if inc_self else None)

    def dma(stream, out, in_):
        q, s = STR[stream]
        if s.n > 0:
            b.wait(q, s, s.n)
        b.emit(q, lambda e, o=out, i=in_: e.dma_start(out=o, in_=i), inc=(s, 16))
        return s.n

    tr_rr = [0]

    def dmat(out, in_):
        snm = f"t{tr_rr[0] % 4}"
        tr_rr[0] += 1
        q, s = STR[snm]
        if s.n > 0:
            b.wait(q, s, s.n)
        b.emit(q, lambda e, o=out, i=in_: e.dma_start(out=o, in_=i, transpose=True),
               inc=(s, 16))
        return s

    def wait_trs(eng):
        for i in range(4):
            b.wait(eng, STR[f"t{i}"][1])

    def barrier():
        snap = {k: v.n for k, v in SEMS.items()}
        dsnap = {k: v[1].n for k, v in STR.items()}
        for eng in ("pe", "act", "dve", "pool", "sp"):
            for k, v in snap.items():
                if k != eng:
                    b.wait(eng, SEMS[k], v)
            for k, vn in dsnap.items():
                b.wait(eng, STR[k][1], vn)

    def finish():
        for name, ap in dbg.items():
            dma("out", ap[:, :], DS[name][:, :])
        for nm, ar, cols in [("AR1", AR1, 33280), ("AR2", AR2, 32768),
                             ("AR3", AR3, 8192)]:
            if nm in taps:
                t = nc.dram_tensor("dbg_" + nm, [128, cols], BF16,
                                   kind="ExternalOutput").ap()
                dma("out", t[:, :], ar[:, 0:cols])
        for nm, tn in [("c_bias", c_bias), ("cn2q", cn2q), ("cn2k", cn2k),
                       ("cn2t", cn2t)]:
            if nm in taps:
                shp = list(tn.shape)
                t = nc.dram_tensor("dbg_" + nm, shp, F32,
                                   kind="ExternalOutput").ap()
                dma("out", t[:, :], tn[:, :])
        barrier()
        with nc.Block() as blk:
            @blk.tensor
            def _(t):
                for f in b.q["pe"]:
                    f(t)

            @blk.scalar
            def _(s):
                for f in b.q["act"]:
                    f(s)

            @blk.vector
            def _(v):
                for f in b.q["dve"]:
                    f(v)

            @blk.gpsimd
            def _(g):
                for f in b.q["pool"]:
                    f(g)

            @blk.sync
            def _(s):
                for f in b.q["sp"]:
                    f(s)
        es.close()
        print("instr counts:", {k: len(v) for k, v in b.q.items()})
        return nc

    def dve_space():
        emit("dve", lambda e: e.tensor_copy(spacer[:, :], spacer[:, :]))
        emit("dve", lambda e: e.tensor_copy(spacer[:, :], spacer[:, :]))
        emit("dve", lambda e: e.tensor_copy(spacer[:, :], spacer[:, :]))

    def ln_group(srcs, outs):
        n = len(srcs)
        for r, s in enumerate(srcs):
            emit("dve", lambda e, s=s, r=r: e.reduce_sum(ln_mu[:, r:r + 1], s, axis=AX.X))
        for r in range(n):
            emit("dve", lambda e, r=r: e.tensor_scalar(
                ln_mu[:, r:r + 1], ln_mu[:, r:r + 1], -1.0 / 1024.0, None, op0=OP.mult),
                inc_self=(r == n - 1))
        b.wait("act", c_dve)
        for r, s in enumerate(srcs):
            emit("act", lambda e, s=s, r=r: e.activation(
                ln_sq[:, :], s, AF.Square, bias=ln_mu[:, r:r + 1], scale=1.0,
                accum_out=ln_var[:, r:r + 1]))
        emit("act", lambda e: e.activation(spacer[:, :], spacer[:, :], AF.Copy,
                                           bias=0.0, scale=1.0))
        for r in range(n):
            emit("act", lambda e, r=r: e.activation(
                ln_rs[:, r:r + 1], ln_var[:, r:r + 1], AF.Sqrt,
                bias=eps_t[:, 0:1], scale=1.0 / 1024.0), inc_self=(r == n - 1))
        b.wait("dve", c_act)
        for r in range(n):
            emit("dve", lambda e, r=r: e.reciprocal(ln_rs[:, r:r + 1], ln_rs[:, r:r + 1]))
        dve_space()
        for r, s in enumerate(srcs):
            for jj, dst in enumerate(outs[r]):
                emit("dve", lambda e, s=s, r=r, dst=dst: e.tensor_scalar(
                    dst, s, ln_mu[:, r:r + 1], ln_rs[:, r:r + 1], op0=OP.add, op1=OP.mult),
                    inc_self=(r == n - 1 and jj == len(outs[r]) - 1))

    class WStream:
        def __init__(self):
            self.par = 0
            self.free = [0, 0]

        def load(self, srcs, width):
            par = self.par
            self.par ^= 1
            b.wait("pool", c_pe, self.free[par])
            for col, src in srcs:
                dma("w%d" % par, WF[par][:, col:col + src.shape[-1]], src)
            b.wait("pool", STR["w%d" % par][1])
            emit("pool", lambda e, par=par, width=width: e.tensor_copy(
                WB[par][:, 0:width], WF[par][:, 0:width]), inc_self=True)
            return par, c_pool.n

        def done(self, par):
            self.free[par] = c_pe.n

    ws = WStream()
    _uid = [0]

    def pn_(s):
        _uid[0] += 1
        return f"{s}_{_uid[0]}"

    r_hif = ln_sq.bitcast(F32)[0:1, 0:512]
    r_bc = trig_c[0:65, 0:512]
    emit("dve", lambda e: e.memset(eps_t[:, :], float(EPS)))
    emit("dve", lambda e: e.memset(spacer[:, :], 0.0))
    emit("dve", lambda e: e.memset(c_bias[:, :], 0.0))
    emit("dve", lambda e: e.memset(c_hideb[:, :], 0.0))
    emit("dve", lambda e: e.memset(cn2t[:, :], 0.0))
    emit("dve", lambda e: e.memset(cn2q[:, :], 0.0))
    emit("dve", lambda e: e.memset(cn2k[:, :], 0.0))
    emit("dve", lambda e: e.memset(att_in[:, :], 0.0))
    emit("dve", lambda e: e.memset(eps30[:, :], 1.0e-30))
    emit("dve", lambda e: e.memset(lnb64[:, :], 44.3614195558365))
    emit("dve", lambda e: e.memset(ln_sq[:, :], 0.0))
    emit("dve", lambda e: e.memset(c_bias[:, :], 0.0))
    emit("dve", lambda e: e.memset(c_hideb[:, :], 0.0))
    emit("dve", lambda e: e.memset(cn2t[:, :], 0.0))
    emit("dve", lambda e: e.memset(cn2q[:, :], 0.0))
    emit("dve", lambda e: e.memset(cn2k[:, :], 0.0))
    emit("dve", lambda e: e.memset(att_in[:, :], 0.0))
    emit("dve", lambda e: e.memset(eps30[:, :], 1.0e-30))
    emit("dve", lambda e: e.memset(lnb64[:, :], 44.3614195558365))
    emit("dve", lambda e: e.memset(ln_sq[:, :], 0.0))
    emit("dve", lambda e: e.memset(ones65b[:, :], 1.0))
    emit("dve", lambda e: e.memset(
        AR1.rearrange("p (kt h c) -> p kt h c", h=16, c=65)[:, :, :, 64:65], 1.0),
        inc_self=True)
    for name, dst in [("bqkv_c", sb_bq), ("ln1w_c", sb_ln1), ("ln2w_c", sb_ln2),
                      ("routb_b", sb_rb), ("hide_b", hide_bc), ("ehost", eh),
                      ("sel2", sel2b)]:
        dma("x", dst[:, :], DI[name][:, :])
    b.wait("dve", STR["x"][1])

    def h1T(c):
        return AR2[:, c * 4096:(c + 1) * 4096]

    g1T = h1T

    P0B = [bf_a, bf_b, att_in[:, 0:1024], att_in[:, 1024:2048]]
    xfree = 0
    for g0 in range(0, NT, 4):
        b.wait("sp", c_dve, xfree)
        vx = 0
        for r in range(4):
            tt = g0 + r
            vx = dma("x", XST[r][:, :], DI["x_perm"][tt * 128:(tt + 1) * 128, :])
        b.wait("dve", STR["x"][1], vx)
        b.wait("act", STR["x"][1], vx)
        wait_trs("dve")
        ln_group([XST[r][:, :] for r in range(4)], [[P0B[r][:, :]] for r in range(4)])
        xfree = c_dve.n
        b.wait("sp", c_dve)
        for r in range(4):
            tt = g0 + r
            for c in range(8):
                dmat(h1T(c)[:, tt * 128:(tt + 1) * 128],
                     P0B[r][:, c * 128:(c + 1) * 128])
    barrier()
    if stop_after == "P0":
        return finish()

    with nc.psum_tensor("ps_qk0", [128, 2048], F32) as pqk0, \
         nc.psum_tensor("ps_qk1", [128, 2048], F32) as pqk1:
        PQK = [pqk0, pqk1]
        pfree = [0, 0]
        for oc in range(16):
            par, wv = ws.load([(i * 128, DI["wqkvT"][i * 128:(i + 1) * 128,
                                                     oc * 128:(oc + 1) * 128])
                               for i in range(8)], 1024)
            b.wait("pe", c_pool, wv)
            for th in range(2):
                b.wait("pe", c_dve, pfree[th])
                for i in range(8):
                    for tg in range(4):
                        t0 = th * 2048 + tg * 512
                        emit("pe", lambda e, par=par, i=i, th=th, tg=tg, t0=t0: e.matmul(
                            PQK[th][:, tg * 512:(tg + 1) * 512],
                            WB[par][:, i * 128:(i + 1) * 128],
                            h1T(i)[:, t0:t0 + 512], start=(i == 0), stop=(i == 7)),
                            inc_self=(i == 7 and tg == 3))
                b.wait("dve", c_pe)
                b.wait("dve", STR["pq"][1])
                scale = 0.125 if oc < 8 else 1.0
                for hh in range(2):
                    emit("dve", lambda e, th=th, hh=hh, oc=oc, scale=scale: e.tensor_scalar(
                        BST[hh][:, :], PQK[th][:, hh * 1024:(hh + 1) * 1024],
                        sb_bq[:, oc:oc + 1], scale, op0=OP.add, op1=OP.mult),
                        inc_self=(hh == 1))
                pfree[th] = c_dve.n
                b.wait("sp", c_dve)
                park = DS["qt_park"] if oc < 8 else DS["kt_park"]
                prow = (oc % 8) * 128
                for hh in range(2):
                    dma("pq", park[prow:prow + 128,
                               th * 2048 + hh * 1024: th * 2048 + (hh + 1) * 1024],
                        BST[hh][:, :])
            ws.done(par)
    barrier()

    if stop_after == "P1a":
        return finish()
    for i in range(8):
        par, wv = ws.load([(0, DI["wqkvT"][i * 128:(i + 1) * 128, 2048:3072])], 1024)
        b.wait("pool", c_pool, wv)
        emit("pool", lambda e, par=par, i=i: e.tensor_copy(
            AR3[:, i * 1024:(i + 1) * 1024], WB[par][:, :]), inc_self=True)
        ws.free[par] = 0
    v_vb = dma("x", st_c[:, :], DI["bqkvV_b"][:, :])
    b.wait("pe", c_pool)
    b.wait("dve", STR["x"][1], v_vb)

    def vbuf(kt):
        return AR1[:, kt * 1040:(kt + 1) * 1040]

    with nc.psum_tensor("ps_v0", [128, 512], F32) as pv0, \
         nc.psum_tensor("ps_v1", [128, 512], F32) as pv1:
        PVp = [pv0, pv1]
        vfree = [0, 0]
        for tt in range(NT):
            for dvg in range(2):
                b.wait("pe", c_dve, vfree[dvg])
            for i in range(8):
                for dvg in range(2):
                    emit("pe", lambda e, tt=tt, i=i, dvg=dvg: e.matmul(
                        PVp[dvg][:, :], h1T(i)[:, tt * 128:(tt + 1) * 128],
                        AR3[:, i * 1024 + dvg * 512: i * 1024 + (dvg + 1) * 512],
                        start=(i == 0), stop=(i == 7)),
                        inc_self=(i == 7 and dvg == 1))
            b.wait("dve", c_pe)
            for dvg in range(2):
                emit("dve", lambda e, tt=tt, dvg=dvg: e.tensor_tensor(
                    vbuf(tt).rearrange("p (h c) -> p h c", c=65)[:, dvg * 8:(dvg + 1) * 8, 0:64],
                    PVp[dvg][:, :].rearrange("p (h c) -> p h c", c=64),
                    st_c[:, dvg * 512:(dvg + 1) * 512].rearrange("p (h c) -> p h c", c=64),
                    op=OP.add), inc_self=(dvg == 1))
                vfree[dvg] = c_dve.n
    barrier()

    if stop_after == "P1b":
        return finish()
    _cbl = int(_os.environ.get("CB_LIM", "9"))

    def compute_cbias(p, kt_ap, kcols, qt_ap, qcols, with_hide):
        with nc.psum_tensor(pn_("ps_n0"), [2, 512], F32) as pn0, \
             nc.psum_tensor(pn_("ps_n1"), [2, 512], F32) as pn1, \
             nc.psum_tensor(pn_("ps_cb0"), [128, 1], F32) as pcb0, \
             nc.psum_tensor(pn_("ps_cb1"), [128, 1], F32) as pcb1:
            PN = [pn0, pn1]
            PCB = [pcb0, pcb1]
            sq_ring = [ln_sq[:, 0:512], ln_sq[:, 512:1024]]
            pn_free = [0, 0]
            sq_free = [0, 0]
            for src_ap, cols, dst in [(qt_ap, qcols, cn2q), (kt_ap, kcols, cn2k)]:
                ntg = cols // 512
                emit("dve", lambda e, dst=dst: e.memset(dst[:, :], 0.0), inc_self=True)
                for tg in range(ntg):
                    par = tg % 2
                    b.wait("act", c_pe, sq_free[par])
                    emit("act", lambda e, src_ap=src_ap, tg=tg, par=par: e.activation(
                        sq_ring[par], src_ap[:, tg * 512:(tg + 1) * 512], AF.Square,
                        bias=0.0, scale=1.0), inc_self=True)
                    b.wait("pe", c_act)
                    b.wait("pe", c_dve, pn_free[par])
                    emit("pe", lambda e, par=par: e.matmul(
                        PN[par][:, :], eh[:, 0:2], sq_ring[par],
                        start=True, stop=True, skip_group_check=True), inc_self=True)
                    sq_free[par] = c_pe.n
                    b.wait("dve", c_pe)
                    emit("dve", lambda e, par=par: e.tensor_copy(
                        trig_c[0:2, par * 512:(par + 1) * 512], PN[par][:, :]))
                    emit("dve", lambda e, par=par: e.reduce_max(
                        cn2t[:, par:par + 1],
                        trig_c[0:2, par * 512:(par + 1) * 512], axis=AX.X))
                    dve_space()
                    emit("dve", lambda e, dst=dst, par=par: e.tensor_tensor(
                        dst[:, 0:1], dst[:, 0:1], cn2t[:, par:par + 1], op=OP.max),
                        inc_self=True)
                    pn_free[par] = c_dve.n
            if _cbl < 2:
                emit("dve", lambda e, p=p: e.memset(c_bias[:, 2 * p:2 * p + 2], -30.0),
                     inc_self=True)
                return
            emit("dve", lambda e: e.tensor_tensor(
                cn2t[:, 0:1], cn2q[:, 0:1], cn2k[:, 0:1], op=OP.mult), inc_self=True)
            b.wait("act", c_dve)
            emit("act", lambda e: e.activation(
                cn2t[:, 1:2], cn2t[:, 0:1], AF.Sqrt, bias=0.0, scale=1.0),
                inc_self=True)
            b.wait("dve", c_act)
            emit("dve", lambda e: e.tensor_scalar(
                cnegb[:, :], cn2t[:, 1:2], -1.0, 30.0, op0=OP.mult, op1=OP.add),
                inc_self=True)
            if _cbl < 3:
                emit("dve", lambda e, p=p: e.memset(c_bias[:, 2 * p:2 * p + 2], -30.0),
                     inc_self=True)
                return
            b.wait("pe", c_dve)
            for h2 in range(2):
                emit("pe", lambda e, h2=h2: e.matmul(
                    PCB[h2][:, :], sel2b[:, h2 * 128:(h2 + 1) * 128], cnegb[:, :],
                    start=True, stop=True, skip_group_check=True), inc_self=True)
            b.wait("dve", c_pe)
            for h2 in range(2):
                emit("dve", lambda e, p=p, h2=h2: e.tensor_copy(
                    c_bias[:, 2 * p + h2:2 * p + h2 + 1], PCB[h2][:, :]),
                    inc_self=True)
                dve_space()
            if with_hide:
                for sl in range(4):
                    for h2 in range(2):
                        emit("dve", lambda e, sl=sl, p=p, h2=h2: e.tensor_tensor(
                            c_hideb[:, sl * 16 + 2 * p + h2: sl * 16 + 2 * p + h2 + 1],
                            c_bias[:, 2 * p + h2:2 * p + h2 + 1],
                            hide_bc[:, sl:sl + 1], op=OP.add), inc_self=True)

    _atp = _os.environ.get("P2_PART", "full")

    def attention(p, kt_ap, qt_ap, n_qg, qg_col, kranges_fn, hide_fn, dst_fn):
        b.wait("pe", c_dve)
        b.wait("pe", c_act)
        b.wait("act", c_dve)
        with nc.psum_tensor(pn_("ps_s0"), [128, 1024], F32) as ps0, \
             nc.psum_tensor(pn_("ps_s1"), [128, 1024], F32) as ps1, \
             nc.psum_tensor(pn_("ps_pa"), [65, 512], F32) as ppa, \
             nc.psum_tensor(pn_("ps_pb"), [65, 512], F32) as ppb, \
             nc.psum_tensor(pn_("ps_rb"), [65, 512], F32) as prb:
            PS = [ps0, ps1]
            PACC = [ppa, ppb]
            acc_free = [0, 0]
            s_free = [0, 0]
            pt_free = [0, 0]
            for qg in range(n_qg):
                q0 = qg_col(qg)
                hs = hide_fn(qg)
                chunks = []
                for (ktile0, nch, tag) in kranges_fn(qg):
                    for c in range(nch):
                        chunks.append((ktile0 + c, tag))
                ntot = len(chunks)
                for h2 in range(2):
                    b.wait("pe", c_dve, acc_free[h2])
                gi0 = 0
                gidx = 0
                while gi0 < ntot:
                    grp = chunks[gi0:gi0 + 2]
                    ng = len(grp)
                    par = gidx % 2
                    for h2 in range(2):
                        b.wait("pe", c_act, s_free[h2])
                    for h2 in range(2):
                        for gi, (kt, tag) in enumerate(grp):
                            emit("pe", lambda e, h2=h2, kt=kt, gi=gi, q0=q0: e.matmul(
                                PS[h2][:, gi * 512:(gi + 1) * 512],
                                kt_ap[h2 * 64:(h2 + 1) * 64, kt * 128:(kt + 1) * 128],
                                qt_ap[h2 * 64:(h2 + 1) * 64, q0:q0 + 512],
                                start=True, stop=True, tile_position=(h2 * 64, 0),
                                skip_group_check=True),
                                inc_self=(h2 == 1 and gi == ng - 1))
                    v_s = c_pe.n
                    if _atp == "s":
                        gi0 += ng
                        gidx += 1
                        continue
                    b.wait("act", c_pe, v_s)
                    for h2 in range(2):
                        b.wait("act", c_pe, pt_free[par])
                        any_diag = any(tag == "diag" for (_, tag) in grp)
                        if any_diag:
                            b.wait("dve", c_pe, v_s)
                            for gi, (kt, tag) in enumerate(grp):
                                if tag == "diag":
                                    cc = kt % 4
                                    emit("dve", lambda e, h2=h2, gi=gi, cc=cc: e.tensor_tensor(
                                        s_mask[:, gi * 512:(gi + 1) * 512],
                                        PS[h2][:, gi * 512:(gi + 1) * 512],
                                        acc32[:, 9216 + cc * 512: 9216 + (cc + 1) * 512],
                                        op=OP.add), inc_self=True)
                            b.wait("act", c_dve)
                        for gi, (kt, tag) in enumerate(grp):
                            col = 2 * p + h2
                            if tag == "hide" and hs is not None:
                                bias_ap = c_hideb[:, hs * 16 + col: hs * 16 + col + 1]
                            else:
                                bias_ap = c_bias[:, col:col + 1]
                            src = (s_mask[:, gi * 512:(gi + 1) * 512] if tag == "diag"
                                   else PS[h2][:, gi * 512:(gi + 1) * 512])
                            emit("act", lambda e, src=src, h2=h2, par=par, gi=gi,
                                 bias_ap=bias_ap: e.activation(
                                     pt_buf[:, h2 * 2048 + par * 1024 + gi * 512:
                                            h2 * 2048 + par * 1024 + (gi + 1) * 512],
                                     src, AF.Exp, bias=bias_ap, scale=1.0),
                                 inc_self=(gi == ng - 1))
                        s_free[h2] = c_act.n
                    v_e = c_act.n
                    if _atp == "se":
                        gi0 += ng
                        gidx += 1
                        continue
                    b.wait("pe", c_act, v_e)
                    for h2 in range(2):
                        for gi, (kt, tag) in enumerate(grp):
                            emit("pe", lambda e, h2=h2, p=p, kt=kt, par=par, gi=gi,
                                 s=(gi0 + gi == 0), st=(gi0 + gi == ntot - 1): e.matmul(
                                     PACC[h2][:, :],
                                     vbuf(kt)[:, (2 * p + h2) * 65:(2 * p + h2) * 65 + 65],
                                     pt_buf[:, h2 * 2048 + par * 1024 + gi * 512:
                                            h2 * 2048 + par * 1024 + (gi + 1) * 512],
                                     start=s, stop=st, skip_group_check=True),
                                 inc_self=(h2 == 1 and gi == ng - 1))
                    pt_free[par] = c_pe.n
                    gi0 += ng
                    gidx += 1
                v_pv = c_pe.n
                if _atp in ("s", "se", "sep"):
                    for h2 in range(2):
                        acc_free[h2] = 0
                    continue
                for h2 in range(2):
                    b.wait("act", c_pe, v_pv)
                    emit("act", lambda e, h2=h2: e.activation(
                        r_hif[:, :], PACC[h2][64:65, :], AF.Ln,
                        bias=eps30[:, :], scale=1.0))
                    emit("act", lambda e: e.activation(
                        spacer[0:1, 0:64], spacer[0:1, 0:64], AF.Copy,
                        bias=0.0, scale=1.0))
                    emit("act", lambda e: e.activation(
                        r_inv[:, :], r_hif[:, :], AF.Exp, bias=0.0,
                        scale=-1.0), inc_self=True)
                    b.wait("dve", c_act)
                    emit("dve", lambda e: e.tensor_copy(r_hi[:, :], r_inv[:, :]))
                    emit("dve", lambda e: e.tensor_copy(r_hif[:, :], r_hi[:, :]))
                    emit("dve", lambda e: e.tensor_tensor(
                        r_hif[:, :], r_inv[:, :], r_hif[:, :], op=OP.subtract))
                    emit("dve", lambda e: e.tensor_copy(r_lo[:, :], r_hif[:, :]),
                         inc_self=True)
                    b.wait("pe", c_dve)
                    emit("pe", lambda e: e.matmul(prb[:, :], ones65b[:, :], r_hi[:, :],
                                                  start=True, stop=False,
                                                  skip_group_check=True))
                    emit("pe", lambda e: e.matmul(prb[:, :], ones65b[:, :], r_lo[:, :],
                                                  start=False, stop=True,
                                                  skip_group_check=True), inc_self=True)
                    b.wait("act", c_pe)
                    emit("act", lambda e: e.activation(r_bc[:, :], prb[:, :], AF.Copy,
                                                       bias=0.0, scale=1.0), inc_self=True)
                    b.wait("dve", c_act)
                    emit("dve", lambda e, h2=h2, qg=qg: e.tensor_tensor(
                        dst_fn(h2, qg), PACC[h2][0:64, :], r_bc[0:64, :], op=OP.mult),
                        inc_self=True)
                    acc_free[h2] = c_dve.n

    _np2 = int(_os.environ.get("P2_NP", "8"))
    _nqh = int(_os.environ.get("P2_NQH", "2"))
    _nqg = int(_os.environ.get("P2_NQG", "4"))
    _nkc = int(_os.environ.get("P2_NKC", "32"))
    _docb = int(_os.environ.get("P2_CB", "1"))
    _doat = int(_os.environ.get("P2_AT", "1"))
    for p in range(_np2):
        b.wait("sp", c_pe)
        b.wait("sp", c_act)
        vq = dma("x", AR3[:, 0:4096], DS["qt_park"][p * 128:(p + 1) * 128, :])
        vk = dma("x", AR3[:, 4096:8192], DS["kt_park"][p * 128:(p + 1) * 128, :])
        b.wait("act", STR["x"][1], vk)
        b.wait("dve", STR["x"][1], vk)
        b.wait("pe", STR["x"][1], vk)
        if _docb:
            compute_cbias(p, AR3[:, 4096:8192], 4096, AR3[:, 0:4096], 4096,
                          with_hide=False)
        else:
            emit("dve", lambda e, p=p: e.memset(c_bias[:, 2 * p:2 * p + 2], -30.0),
                 inc_self=True)
        if not _doat:
            continue
        for qh in range(_nqh):
            attention(p, AR3[:, 4096:8192], AR3[:, 0:4096], _nqg,
                      qg_col=lambda qg, qh=qh: qh * 2048 + qg * 512,
                      kranges_fn=lambda qg: [(0, _nkc, None)],
                      hide_fn=lambda qg: None,
                      dst_fn=lambda h2, qg: att_in[h2 * 64:(h2 + 1) * 64,
                                                   qg * 512:(qg + 1) * 512])
            b.wait("sp", c_dve)
            dma("pa", DS["att_park"][p * 128:p * 128 + 64, qh * 2048:(qh + 1) * 2048],
                att_in[0:64, :])
            v = dma("pa", DS["att_park"][p * 128 + 64:(p + 1) * 128,
                                         qh * 2048:(qh + 1) * 2048], att_in[64:128, :])
            b.wait("dve", STR["pa"][1], v)
    barrier()

    if stop_after == "P2":
        return finish()
    for i in range(8):
        par, wv = ws.load([(0, DI["woT"][i * 128:(i + 1) * 128, :])], 1024)
        b.wait("pool", c_pool, wv)
        emit("pool", lambda e, par=par, i=i: e.tensor_copy(
            AR3[:, i * 1024:(i + 1) * 1024], WB[par][:, :]), inc_self=True)
        ws.free[par] = 0
    v_bo = dma("x", trig_c[:, 0:1024], DI["bo_b"][:, :])
    b.wait("pe", c_pool)
    b.wait("dve", STR["x"][1], v_bo)

    with nc.psum_tensor("ps_o0", [128, 512], F32) as po0, \
         nc.psum_tensor("ps_o1", [128, 512], F32) as po1:
        PO = [po0, po1]
        ofree = [0, 0]
        att_free = 0
        for tt in range(NT):
            own = tt < 16
            b.wait("sp", c_pe, att_free)
            b.wait("sp", c_dve)
            for fc in range(8):
                va = dma("x", att_in[:, fc * 128:(fc + 1) * 128],
                         DS["att_park"][fc * 128:(fc + 1) * 128, tt * 128:(tt + 1) * 128])
            b.wait("sp", STR["pf"][1])
            vx = dma("x", XST[tt % 2][:, :], DI["x_perm"][tt * 128:(tt + 1) * 128, :])
            b.wait("pe", STR["x"][1], va)
            for og in range(2):
                b.wait("pe", c_dve, ofree[og])
            for fc in range(8):
                for og in range(2):
                    emit("pe", lambda e, fc=fc, og=og: e.matmul(
                        PO[og][:, :], att_in[:, fc * 128:(fc + 1) * 128],
                        AR3[:, fc * 1024 + og * 512: fc * 1024 + (og + 1) * 512],
                        start=(fc == 0), stop=(fc == 7)),
                        inc_self=(fc == 7 and og == 1))
            att_free = c_pe.n
            b.wait("dve", c_pe)
            b.wait("dve", STR["pf"][1])
            st_o = XST[2 + tt % 2]
            for og in range(2):
                emit("dve", lambda e, og=og, st_o=st_o: e.tensor_tensor(
                    st_o[:, og * 512:(og + 1) * 512], PO[og][:, :],
                    trig_c[:, og * 512:(og + 1) * 512], op=OP.add),
                    inc_self=(og == 1))
                ofree[og] = c_dve.n
            b.wait("dve", STR["x"][1], vx)
            st_x = XST[tt % 2]
            ln_group([st_o[:, :]], [[st_o[:, :]]])
            emit("dve", lambda e, st_x=st_x, st_o=st_o: e.tensor_tensor(
                st_x[:, :], st_x[:, :], st_o[:, :], op=OP.add), inc_self=True)
            if own:
                b.wait("sp", c_dve)
                dma("pf", DS["x1_park"][tt * 128:(tt + 1) * 128, :], st_x[:, :])
            ln_group([st_x[:, :]], [[st_o[:, :]]])
            if own:
                b.wait("sp", c_dve)
                dma("pf", DS["f_park"][tt * 128:(tt + 1) * 128, :], st_o[:, :])
            wait_trs("dve")
            ln_group([st_o[:, :]], [[BST[tt % 2][:, :]]])
            b.wait("sp", c_dve)
            for c in range(8):
                dmat(g1T(c)[:, tt * 128:(tt + 1) * 128],
                     BST[tt % 2][:, c * 128:(c + 1) * 128])
    barrier()
    for c in range(8):
        emit("act", lambda e, c=c: e.activation(
            g1T(c)[:, :], g1T(c)[:, :], AF.Copy, bias=0.0, scale=sb_ln1[:, c:c + 1]),
            inc_self=True)
    barrier()
    if stop_after == "P3":
        return finish()

    def kvlatT(c):
        return AR3[:, c * 4096:(c + 1) * 4096]

    def qlatT(c):
        return AR3[:, 8192 + c * 2048: 8192 + (c + 1) * 2048]

    with nc.psum_tensor("ps_l", [128, 2048], F32) as pl:
        lfree = [0]
        for wname, dstf, ntcols in [("kvdT", kvlatT, 4096), ("qdT", qlatT, 2048)]:
            for lc in range(2):
                par, wv = ws.load([(i * 128, DI[wname][i * 128:(i + 1) * 128,
                                                       lc * 128:(lc + 1) * 128])
                                   for i in range(8)], 1024)
                b.wait("pe", c_pool, wv)
                for thg in range(ntcols // 2048):
                    b.wait("pe", c_act, lfree[0])
                    for i in range(8):
                        for tg in range(4):
                            t0 = thg * 2048 + tg * 512
                            emit("pe", lambda e, par=par, i=i, tg=tg, t0=t0: e.matmul(
                                pl[:, tg * 512:(tg + 1) * 512],
                                WB[par][:, i * 128:(i + 1) * 128],
                                g1T(i)[:, t0:t0 + 512], start=(i == 0), stop=(i == 7)),
                                inc_self=(i == 7 and tg == 3))
                    b.wait("act", c_pe)
                    for hh in range(2):
                        emit("act", lambda e, dstf=dstf, lc=lc, thg=thg, hh=hh: e.activation(
                            dstf(lc)[:, thg * 2048 + hh * 1024: thg * 2048 + (hh + 1) * 1024],
                            pl[:, hh * 1024:(hh + 1) * 1024], AF.Copy, bias=0.0, scale=1.0),
                            inc_self=(hh == 1))
                    lfree[0] = c_act.n
                ws.done(par)
    barrier()
    if stop_after == "P4a":
        return finish()

    def head_slice_dma(stage_ap, park, oc, tcols0, ncols, row_off):
        for s in range(4):
            h = oc * 4 + s
            dma("pk", park[h * 64 + row_off: h * 64 + row_off + 32,
                           tcols0:tcols0 + ncols],
                stage_ap[s * 32:(s + 1) * 32, :])
        return STR["pk"][1].n

    with nc.psum_tensor("ps_k0", [128, 2048], F32) as pk0:
        kfree = [0]
        pk_last = [0]
        for wname, latf, park, ntcols, row_off, scale in [
                ("kuT", kvlatT, DS["kt2_park"], 4096, 32, 1.0),
                ("quT", qlatT, DS["qt2_park"], 2048, 32, 0.125)]:
            for oc in range(4):
                par, wv = ws.load([(lc * 128, DI[wname][lc * 128:(lc + 1) * 128,
                                                        oc * 128:(oc + 1) * 128])
                                   for lc in range(2)], 256)
                b.wait("pe", c_pool, wv)
                for thg in range(ntcols // 2048):
                    b.wait("pe", c_act, kfree[0])
                    for lc in range(2):
                        for tg in range(4):
                            t0 = thg * 2048 + tg * 512
                            emit("pe", lambda e, par=par, lc=lc, tg=tg, t0=t0, latf=latf:
                                 e.matmul(pk0[:, tg * 512:(tg + 1) * 512],
                                          WB[par][:, lc * 128:(lc + 1) * 128],
                                          latf(lc)[:, t0:t0 + 512],
                                          start=(lc == 0), stop=(lc == 1)),
                                 inc_self=(lc == 1 and tg == 3))
                    b.wait("act", c_pe)
                    b.wait("act", STR["pk"][1], pk_last[0])
                    for hh in range(2):
                        emit("act", lambda e, hh=hh, scale=scale: e.activation(
                            BST[hh][:, :], pk0[:, hh * 1024:(hh + 1) * 1024], AF.Copy,
                            bias=0.0, scale=scale), inc_self=(hh == 1))
                    kfree[0] = c_act.n
                    b.wait("sp", c_act)
                    for hh in range(2):
                        pk_last[0] = head_slice_dma(BST[hh][:, :], park, oc,
                                                    thg * 2048 + hh * 1024, 1024,
                                                    row_off)
                ws.done(par)
    barrier()

    if stop_after == "P4b":
        return finish()
    with nc.psum_tensor("ps_r0", [128, 512], F32) as pr0, \
         nc.psum_tensor("ps_r1", [128, 512], F32) as pr1:
        rfree = [0]
        for wname, wsname, latf, nlc, park, ntcols, ckey, skey in [
                ("rkT", "rksT", g1T, 8, DS["kt2_park"], 4096, "cosk", "sink"),
                ("rqT", "rqsT", qlatT, 2, DS["qt2_park"], 2048, "cosq", "sinq")]:
            for oc in range(4):
                par, wv = ws.load(
                    [(i * 128, DI[wname][i * 128:(i + 1) * 128, oc * 128:(oc + 1) * 128])
                     for i in range(nlc)] +
                    [(512 + i * 128, DI[wsname][i * 128:(i + 1) * 128,
                                                oc * 128:(oc + 1) * 128])
                     for i in range(nlc)] if nlc == 2 else
                    [(i * 128, DI[wname][i * 128:(i + 1) * 128, oc * 128:(oc + 1) * 128])
                     for i in range(8)], 1024 if nlc == 8 else 1024)
                if nlc == 8:
                    par2, wv2 = ws.load(
                        [(i * 128, DI[wsname][i * 128:(i + 1) * 128,
                                              oc * 128:(oc + 1) * 128])
                         for i in range(8)], 1024)
                else:
                    par2, wv2 = par, wv
                b.wait("pe", c_pool, max(wv, wv2))
                for tg in range(ntcols // 512):
                    b.wait("sp", c_dve, rfree[0])
                    vtr = dma("x", trig_c[:, 0:512], DI[ckey][:, tg * 512:(tg + 1) * 512])
                    vtr = dma("x", trig_c[:, 512:1024], DI[skey][:, tg * 512:(tg + 1) * 512])
                    b.wait("pe", c_dve, rfree[0])
                    for i in range(nlc):
                        off = (i * 128) if nlc == 8 else (i * 128)
                        emit("pe", lambda e, par=par, off=off, i=i, tg=tg, latf=latf, nlc=nlc:
                             e.matmul(pr0[:, :], WB[par][:, off:off + 128],
                                      latf(i)[:, tg * 512:(tg + 1) * 512],
                                      start=(i == 0), stop=(i == nlc - 1)))
                    for i in range(nlc):
                        off = (512 + i * 128) if nlc == 2 else (i * 128)
                        emit("pe", lambda e, par2=par2, off=off, i=i, tg=tg, latf=latf,
                             nlc=nlc: e.matmul(pr1[:, :], WB[par2][:, off:off + 128],
                                               latf(i)[:, tg * 512:(tg + 1) * 512],
                                               start=(i == 0), stop=(i == nlc - 1)),
                             inc_self=(i == nlc - 1))
                    b.wait("dve", c_pe)
                    b.wait("dve", STR["x"][1], vtr)
                    emit("dve", lambda e: e.tensor_tensor(
                        s_mask[:, 0:512], pr0[:, :], trig_c[:, 0:512], op=OP.mult))
                    emit("dve", lambda e: e.tensor_tensor(
                        s_mask[:, 512:1024], pr1[:, :], trig_c[:, 512:1024], op=OP.mult),
                        inc_self=True)
                    emit("dve", lambda e: e.tensor_tensor(
                        BST[0][:, 0:512], s_mask[:, 0:512], s_mask[:, 512:1024],
                        op=OP.add), inc_self=True)
                    rfree[0] = c_dve.n
                    b.wait("sp", c_dve)
                    for s in range(4):
                        h = oc * 4 + s
                        scat_v = dma("pk", park[h * 64: h * 64 + 32,
                                                tg * 512:(tg + 1) * 512],
                                     BST[0][s * 32:(s + 1) * 32, 0:512])
                    b.wait("dve", STR["pk"][1], scat_v)
                ws.done(par)
                if nlc == 8:
                    ws.done(par2)
    barrier()
    if stop_after == "P4c":
        return finish()

    vu_par = []
    for lc in range(2):
        par, wv = ws.load([(0, DI["vuT"][lc * 128:(lc + 1) * 128, :])], 1024)
        vu_par.append(par)
    b.wait("pe", c_pool)
    with nc.psum_tensor("ps_v20", [128, 512], F32) as pw0, \
         nc.psum_tensor("ps_v21", [128, 512], F32) as pw1:
        PW = [pw0, pw1]
        vfree = [0, 0]
        for tt in range(NT):
            for dvg in range(2):
                b.wait("pe", c_dve, vfree[dvg])
            for lc in range(2):
                for dvg in range(2):
                    emit("pe", lambda e, lc=lc, tt=tt, dvg=dvg: e.matmul(
                        PW[dvg][:, :], kvlatT(lc)[:, tt * 128:(tt + 1) * 128],
                        WB[vu_par[lc]][:, dvg * 512:(dvg + 1) * 512],
                        start=(lc == 0), stop=(lc == 1)),
                        inc_self=(lc == 1 and dvg == 1))
            b.wait("dve", c_pe)
            for dvg in range(2):
                emit("dve", lambda e, tt=tt, dvg=dvg: e.tensor_copy(
                    vbuf(tt).rearrange("p (h c) -> p h c", c=65)[:, dvg * 8:(dvg + 1) * 8, 0:64],
                    PW[dvg][:, :].rearrange("p (h c) -> p h c", c=64)),
                    inc_self=(dvg == 1))
                vfree[dvg] = c_dve.n
    barrier()

    if stop_after == "P4d":
        return finish()
    v = dma("x", acc32[:, 9216:11264], DI["mask_diag"][:, :])
    b.wait("dve", STR["x"][1], v)

    def attT2(h2, p, qg):
        return AR2[h2 * 64:(h2 + 1) * 64, p * 2048 + qg * 512: p * 2048 + (qg + 1) * 512]

    def mla_kranges(qg):
        r = [(0, 4 * qg, None)] if qg > 0 else []
        r.append((4 * qg, 4, "diag"))
        po = PAD_OTHER[qg]
        if po > 1:
            r.append((16, 4 * (po - 1), None))
        r.append((16 + 4 * (po - 1), 4, "hide"))
        return r

    qt2_sb = AR2[:, 16384:18432]
    for p in range(8):
        b.wait("sp", c_pe)
        b.wait("sp", c_act)
        vk = dma("x", AR3[:, 0:4096], DS["kt2_park"][p * 128:(p + 1) * 128, :])
        vq = dma("x", qt2_sb, DS["qt2_park"][p * 128:(p + 1) * 128, :])
        b.wait("act", STR["x"][1], vq)
        b.wait("dve", STR["x"][1], vq)
        b.wait("pe", STR["x"][1], vq)
        compute_cbias(p, AR3[:, 0:4096], 4096, qt2_sb, 2048, with_hide=True)
        attention(p, AR3[:, 0:4096], qt2_sb, 4,
                  qg_col=lambda qg: qg * 512,
                  kranges_fn=mla_kranges,
                  hide_fn=lambda qg: qg,
                  dst_fn=lambda h2, qg, p=p: attT2(h2, p, qg))
    barrier()
    if stop_after == "P5":
        return finish()

    for i in range(8):
        par, wv = ws.load([(0, DI["owT"][i * 128:(i + 1) * 128, :])], 1024)
        b.wait("pool", c_pool, wv)
        emit("pool", lambda e, par=par, i=i: e.tensor_copy(
            AR3[:, i * 1024:(i + 1) * 1024], WB[par][:, :]), inc_self=True)
        ws.free[par] = 0
    b.wait("pe", c_pool)
    with nc.psum_tensor("ps_m0", [128, 512], F32) as pm0, \
         nc.psum_tensor("ps_m1", [128, 512], F32) as pm1:
        PM = [pm0, pm1]
        mfree = [0, 0]
        for tt in range(16):
            b.wait("sp", STR["pf"][1])
            vf = dma("x", XST[tt % 2][:, :], DS["f_park"][tt * 128:(tt + 1) * 128, :])
            for og in range(2):
                b.wait("pe", c_dve, mfree[og])
            for fc in range(8):
                for og in range(2):
                    emit("pe", lambda e, fc=fc, og=og, tt=tt: e.matmul(
                        PM[og][:, :],
                        AR2[:, fc * 2048 + tt * 128: fc * 2048 + tt * 128 + 128],
                        AR3[:, fc * 1024 + og * 512: fc * 1024 + (og + 1) * 512],
                        start=(fc == 0), stop=(fc == 7)),
                        inc_self=(fc == 7 and og == 1))
            b.wait("dve", c_pe)
            b.wait("dve", STR["x"][1], vf)
            for og in range(2):
                emit("dve", lambda e, og=og, tt=tt: e.tensor_tensor(
                    XST[tt % 2][:, og * 512:(og + 1) * 512], PM[og][:, :],
                    XST[tt % 2][:, og * 512:(og + 1) * 512], op=OP.add),
                    inc_self=(og == 1))
                mfree[og] = c_dve.n
            b.wait("sp", c_dve)
            dma("pf", DS["f_park"][tt * 128:(tt + 1) * 128, :], XST[tt % 2][:, :])
    barrier()
    if stop_after == "P6":
        return finish()

    for i in range(8):
        dma("x", rt_f[:, i * 7:(i + 1) * 7], DI["routT"][i * 128:(i + 1) * 128, :])
    b.wait("dve", STR["x"][1])
    emit("dve", lambda e: e.tensor_copy(rt_hi[:, :], rt_f[:, :]), inc_self=True)
    emit("dve", lambda e: e.memset(spacer[:, :], 0.0))
    emit("dve", lambda e: e.memset(c_bias[:, :], 0.0))
    emit("dve", lambda e: e.memset(c_hideb[:, :], 0.0))
    emit("dve", lambda e: e.memset(cn2t[:, :], 0.0))
    emit("dve", lambda e: e.memset(cn2q[:, :], 0.0))
    emit("dve", lambda e: e.memset(cn2k[:, :], 0.0))
    emit("dve", lambda e: e.memset(att_in[:, :], 0.0))
    emit("dve", lambda e: e.memset(eps30[:, :], 1.0e-30))
    emit("dve", lambda e: e.memset(lnb64[:, :], 44.3614195558365))
    emit("dve", lambda e: e.memset(ln_sq[:, :], 0.0))
    emit("dve", lambda e: e.tensor_copy(rt_f2[:, :], rt_hi[:, :]))
    emit("dve", lambda e: e.memset(spacer[:, :], 0.0))
    emit("dve", lambda e: e.memset(c_bias[:, :], 0.0))
    emit("dve", lambda e: e.memset(c_hideb[:, :], 0.0))
    emit("dve", lambda e: e.memset(cn2t[:, :], 0.0))
    emit("dve", lambda e: e.memset(cn2q[:, :], 0.0))
    emit("dve", lambda e: e.memset(cn2k[:, :], 0.0))
    emit("dve", lambda e: e.memset(att_in[:, :], 0.0))
    emit("dve", lambda e: e.memset(eps30[:, :], 1.0e-30))
    emit("dve", lambda e: e.memset(lnb64[:, :], 44.3614195558365))
    emit("dve", lambda e: e.memset(ln_sq[:, :], 0.0))
    emit("dve", lambda e: e.tensor_tensor(rt_f2[:, :], rt_f[:, :], rt_f2[:, :],
                                          op=OP.subtract))
    emit("dve", lambda e: e.memset(spacer[:, :], 0.0))
    emit("dve", lambda e: e.memset(c_bias[:, :], 0.0))
    emit("dve", lambda e: e.memset(c_hideb[:, :], 0.0))
    emit("dve", lambda e: e.memset(cn2t[:, :], 0.0))
    emit("dve", lambda e: e.memset(cn2q[:, :], 0.0))
    emit("dve", lambda e: e.memset(cn2k[:, :], 0.0))
    emit("dve", lambda e: e.memset(att_in[:, :], 0.0))
    emit("dve", lambda e: e.memset(eps30[:, :], 1.0e-30))
    emit("dve", lambda e: e.memset(lnb64[:, :], 44.3614195558365))
    emit("dve", lambda e: e.memset(ln_sq[:, :], 0.0))
    emit("dve", lambda e: e.tensor_copy(rt_lo[:, :], rt_f2[:, :]), inc_self=True)

    def xt_c(c):
        return AR3[:, c * 1024:(c + 1) * 1024]

    def xlo_c(c):
        return AR1[:, 16384 + c * 1024: 16384 + (c + 1) * 1024]

    def h_c(c):
        return AR1[:, c * 1024:(c + 1) * 1024]

    def dtile(i2):
        return AR1[:, 16384 + i2 * 1024: 16384 + (i2 + 1) * 1024]

    P0B2 = [bf_a, bf_b, att_in[:, 0:1024], att_in[:, 1024:2048]]
    DST32 = [st_c, st_d]

    for hf in range(2):
        for g0 in range(0, 8, 4):
            b.wait("sp", c_dve)
            vx = 0
            for r in range(4):
                tl = g0 + r
                row = (hf * 8 + tl) * 128
                vx = dma("x", XST[r][:, :], DS["f_park"][row:row + 128, :])
            b.wait("dve", STR["x"][1], vx)
            b.wait("act", STR["x"][1], vx)
            ln_group([XST[r][:, :] for r in range(4)],
                     [[P0B2[r][:, :], XST[r][:, :]] for r in range(4)])
            for r in range(4):
                tl = g0 + r
                emit("dve", lambda e, r=r: e.tensor_copy(s_mask[:, :], P0B2[r][:, :]))
                emit("dve", lambda e, r=r: e.tensor_tensor(
                    ln_sq[:, :], XST[r][:, :], s_mask[:, :], op=OP.subtract),
                    inc_self=True)
                b.wait("sp", c_dve)
                for c in range(8):
                    dmat(xt_c(c)[:, tl * 128:(tl + 1) * 128],
                         P0B2[r][:, c * 128:(c + 1) * 128])
                for c in range(8):
                    dmat(xlo_c(c)[:, tl * 128:(tl + 1) * 128],
                         ln_sq[:, c * 128:(c + 1) * 128])
                wait_trs("dve")
        barrier()
        for c in range(8):
            emit("act", lambda e, c=c: e.activation(
                xt_c(c)[:, :], xt_c(c)[:, :], AF.Copy, bias=0.0,
                scale=sb_ln2[:, c:c + 1]))
            emit("act", lambda e, c=c: e.activation(
                xlo_c(c)[:, :], xlo_c(c)[:, :], AF.Copy, bias=0.0,
                scale=sb_ln2[:, c:c + 1]), inc_self=(c == 7))
        barrier()

        with nc.psum_tensor(pn_("ps_r7"), [128, 7], F32) as pr7:
            pr_free = 0
            for tl in range(8):
                b.wait("pe", c_dve, pr_free)
                seq = []
                for c in range(8):
                    seq.append((xt_c(c), None))
                    seq.append((xt_c(c), rt_hi, c))
                    seq.append((xt_c(c), rt_lo, c))
                for c in range(8):
                    seq.append((xlo_c(c), None))
                    seq.append((xlo_c(c), rt_hi, c))
                nmm = sum(1 for s in seq if len(s) == 3)
                k = 0
                for item in seq:
                    if len(item) == 2:
                        xc = item[0]
                    else:
                        xc, rt, c = item
                        k += 1
                        emit("pe", lambda e, xc=xc, tl=tl, rt=rt, c=c,
                             s=(k == 1), t=(k == nmm): e.matmul(
                                 pr7[:, :], xc[:, tl * 128:(tl + 1) * 128],
                                 rt[:, c * 7:(c + 1) * 7], start=s, stop=t,
                                 skip_group_check=True),
                             inc_self=(k == nmm))
                b.wait("dve", c_pe)
                emit("dve", lambda e, tl=tl: e.tensor_tensor(
                    prob_b[:, tl * 7:(tl + 1) * 7], pr7[:, :], sb_rb[:, :],
                    op=OP.add), inc_self=True)
                pr_free = c_dve.n
        b.wait("act", c_dve)
        emit("act", lambda e: e.activation(prob_b[:, :], prob_b[:, :], AF.Sigmoid,
                                           bias=0.0, scale=1.0), inc_self=True)
        b.wait("dve", c_act)
        for tl in range(8):
            emit("dve", lambda e, tl=tl: e.reduce_max(
                tp_m[:, tl:tl + 1], prob_b[:, tl * 7:(tl + 1) * 7], axis=AX.X),
                inc_self=(tl == 7))
        b.wait("act", c_dve)
        for tl in range(8):
            emit("act", lambda e, tl=tl: e.activation(
                tp_sg[:, tl * 7:(tl + 1) * 7], prob_b[:, tl * 7:(tl + 1) * 7],
                AF.Sign, bias=tp_m[:, tl:tl + 1], scale=-1.0), inc_self=(tl == 7))
        b.wait("dve", c_act)
        for tl in range(8):
            emit("dve", lambda e, tl=tl: e.tensor_scalar(
                tp_sg[:, tl * 7:(tl + 1) * 7], tp_sg[:, tl * 7:(tl + 1) * 7],
                -1.0, 1.0, op0=OP.mult, op1=OP.add))
        for tl in range(8):
            emit("dve", lambda e, tl=tl: e.scalar_tensor_tensor(
                tp_p2[:, tl * 7:(tl + 1) * 7], tp_sg[:, tl * 7:(tl + 1) * 7],
                -1.0e9, prob_b[:, tl * 7:(tl + 1) * 7], op0=OP.mult, op1=OP.add))
        for tl in range(8):
            emit("dve", lambda e, tl=tl: e.reduce_max(
                tp_m[:, tl:tl + 1], tp_p2[:, tl * 7:(tl + 1) * 7], axis=AX.X),
                inc_self=(tl == 7))
        b.wait("act", c_dve)
        for tl in range(8):
            emit("act", lambda e, tl=tl: e.activation(
                rt_f2[:, tl * 7:(tl + 1) * 7], tp_p2[:, tl * 7:(tl + 1) * 7],
                AF.Sign, bias=tp_m[:, tl:tl + 1], scale=-1.0), inc_self=(tl == 7))
        b.wait("dve", c_act)
        for tl in range(8):
            emit("dve", lambda e, tl=tl: e.tensor_scalar(
                rt_f2[:, tl * 7:(tl + 1) * 7], rt_f2[:, tl * 7:(tl + 1) * 7],
                -1.0, 1.0, op0=OP.mult, op1=OP.add))
        for tl in range(8):
            emit("dve", lambda e, tl=tl: e.tensor_tensor(
                tp_sg[:, tl * 7:(tl + 1) * 7], tp_sg[:, tl * 7:(tl + 1) * 7],
                rt_f2[:, tl * 7:(tl + 1) * 7], op=OP.add))
        for tl in range(8):
            emit("dve", lambda e, tl=tl: e.tensor_tensor(
                tp_p2[:, tl * 7:(tl + 1) * 7], prob_b[:, tl * 7:(tl + 1) * 7],
                tp_sg[:, tl * 7:(tl + 1) * 7], op=OP.mult))
        for tl in range(8):
            emit("dve", lambda e, tl=tl: e.reduce_sum(
                tp_den[:, tl:tl + 1], tp_p2[:, tl * 7:(tl + 1) * 7], axis=AX.X))
        dve_space()
        for tl in range(8):
            emit("dve", lambda e, tl=tl: e.reciprocal(
                tp_den[:, tl:tl + 1], tp_den[:, tl:tl + 1]))
        dve_space()
        for tl in range(8):
            emit("dve", lambda e, tl=tl: e.tensor_scalar(
                wgt_b[:, tl * 7:(tl + 1) * 7], tp_p2[:, tl * 7:(tl + 1) * 7],
                tp_den[:, tl:tl + 1], None, op0=OP.mult), inc_self=(tl == 7))

        with nc.psum_tensor(pn_("ps_g"), [128, 1024], F32) as pg, \
             nc.psum_tensor(pn_("ps_u"), [128, 1024], F32) as pu, \
             nc.psum_tensor(pn_("ps_d0"), [128, 512], F32) as pd0, \
             nc.psum_tensor(pn_("ps_d1"), [128, 512], F32) as pd1:
            PD = [pd0, pd1]
            b.wait("pe", c_dve)
            b.wait("pe", c_act)
            down_done = c_pe.n
            g_free = 0
            u_free = 0
            h_free = [0] * 16
            acc_free = 0
            for e_i in range(8):
                b.wait("pool", c_pe, down_done)
                dvals = {}
                dvals[0] = dma("d0", DST32[0][:, :], DI["downT"][e_i, 0:128, :])
                for i2 in range(16):
                    tp = i2 % 2
                    if i2 < 15:
                        dvals[i2 + 1] = dma("d%d" % (1 - tp), DST32[1 - tp][:, :],
                                            DI["downT"][e_i,
                                                        (i2 + 1) * 128:(i2 + 2) * 128, :])
                    b.wait("pool", STR["d%d" % tp][1], dvals[i2])
                    emit("pool", lambda e, i2=i2, tp=tp: e.tensor_copy(
                        dtile(i2)[:, :], DST32[tp][:, :]), inc_self=True)
                d_ready = c_pool.n
                for oc in range(16):
                    parg, wvg = ws.load([(i * 128,
                                          DI["gateT"][e_i, i * 128:(i + 1) * 128,
                                                      oc * 128:(oc + 1) * 128])
                                         for i in range(8)], 1024)
                    b.wait("pe", c_pool, wvg)
                    b.wait("pe", c_act, g_free)
                    for i in range(8):
                        for tg in range(2):
                            emit("pe", lambda e, parg=parg, i=i, tg=tg, hf=hf: e.matmul(
                                pg[:, tg * 512:(tg + 1) * 512],
                                WB[parg][:, i * 128:(i + 1) * 128],
                                xt_c(i)[:, tg * 512:(tg + 1) * 512],
                                start=(i == 0), stop=(i == 7)),
                                inc_self=(i == 7 and tg == 1))
                    ws.done(parg)
                    v_g = c_pe.n
                    paru, wvu = ws.load([(i * 128,
                                          DI["upT"][e_i, i * 128:(i + 1) * 128,
                                                    oc * 128:(oc + 1) * 128])
                                         for i in range(8)], 1024)
                    b.wait("pe", c_pool, wvu)
                    b.wait("pe", c_dve, u_free)
                    for i in range(8):
                        for tg in range(2):
                            emit("pe", lambda e, paru=paru, i=i, tg=tg: e.matmul(
                                pu[:, tg * 512:(tg + 1) * 512],
                                WB[paru][:, i * 128:(i + 1) * 128],
                                xt_c(i)[:, tg * 512:(tg + 1) * 512],
                                start=(i == 0), stop=(i == 7)),
                                inc_self=(i == 7 and tg == 1))
                    ws.done(paru)
                    v_u = c_pe.n
                    b.wait("act", c_pe, v_g)
                    emit("act", lambda e: e.activation(ln_sq[:, :], pg[:, :], AF.Sigmoid,
                                                       bias=0.0, scale=1.0),
                         inc_self=True)
                    b.wait("dve", c_act)
                    emit("dve", lambda e: e.tensor_tensor(
                        st_a[:, :], pg[:, :], ln_sq[:, :], op=OP.mult), inc_self=True)
                    g_free = c_dve.n
                    b.wait("dve", c_pe, v_u)
                    b.wait("dve", c_pe, h_free[oc])
                    emit("dve", lambda e, oc=oc: e.tensor_tensor(
                        h_c(oc)[:, :], st_a[:, :], pu[:, :], op=OP.mult), inc_self=True)
                    u_free = c_dve.n
                b.wait("pe", c_pool, d_ready)
                b.wait("pe", c_dve, u_free)
                for tl in range(8):
                    b.wait("pe", c_dve, acc_free)
                    for i2 in range(16):
                        for og in range(2):
                            emit("pe", lambda e, i2=i2, tl=tl, og=og: e.matmul(
                                PD[og][:, :], h_c(i2)[:, tl * 128:(tl + 1) * 128],
                                dtile(i2)[:, og * 512:(og + 1) * 512],
                                start=(i2 == 0), stop=(i2 == 15)),
                                inc_self=(i2 == 15 and og == 1))
                    b.wait("dve", c_pe)
                    for og in range(2):
                        if e_i == 0:
                            emit("dve", lambda e, tl=tl, og=og: e.tensor_copy(
                                acc32[:, tl * 1024 + og * 512: tl * 1024 + (og + 1) * 512],
                                PD[og][:, :]), inc_self=(og == 1))
                        else:
                            emit("dve", lambda e, tl=tl, og=og, e_i=e_i:
                                 e.scalar_tensor_tensor(
                                     acc32[:, tl * 1024 + og * 512:
                                           tl * 1024 + (og + 1) * 512],
                                     PD[og][:, :], wgt_b[:, tl * 7 + e_i - 1:
                                                         tl * 7 + e_i], 
                                     acc32[:, tl * 1024 + og * 512:
                                           tl * 1024 + (og + 1) * 512],
                                     op0=OP.mult, op1=OP.add), inc_self=(og == 1))
                    acc_free = c_dve.n
                for oc in range(16):
                    h_free[oc] = c_pe.n
                down_done = c_pe.n

        barrier()
        for g0 in range(0, 8, 4):
            b.wait("sp", c_dve)
            b.wait("sp", STR["out"][1])
            vx = 0
            for r in range(4):
                tl = g0 + r
                row = (hf * 8 + tl) * 128
                vx = dma("x", XST[r][:, :], DS["f_park"][row:row + 128, :])
            b.wait("dve", STR["x"][1], vx)
            for r in range(4):
                tl = g0 + r
                emit("dve", lambda e, r=r, tl=tl: e.tensor_tensor(
                    XST[r][:, :], XST[r][:, :], acc32[:, tl * 1024:(tl + 1) * 1024],
                    op=OP.add), inc_self=(r == 3))
            b.wait("act", c_dve)
            ln_group([XST[r][:, :] for r in range(4)],
                     [[acc32[:, (g0 + r) * 1024:(g0 + r + 1) * 1024]] for r in range(4)])
            b.wait("sp", c_dve)
            vx = 0
            for r in range(4):
                tl = g0 + r
                row = (hf * 8 + tl) * 128
                vx = dma("x", XST[r][:, :], DS["x1_park"][row:row + 128, :])
            b.wait("dve", STR["x"][1], vx)
            for r in range(4):
                tl = g0 + r
                emit("dve", lambda e, r=r, tl=tl: e.tensor_tensor(
                    XST[r][:, :], XST[r][:, :], acc32[:, tl * 1024:(tl + 1) * 1024],
                    op=OP.add), inc_self=True)
                b.wait("sp", c_dve)
                row = (hf * 8 + tl) * 128
                dma("out", out_d[row:row + 128, :], XST[r][:, :])
        barrier()

    return finish()


def kernel(**inputs):
    nc = build_program()
    in_maps = []
    row_maps = []
    for core in range(8):
        bb, g = core // 2, core % 2
        d, rows = _prep_core_inputs(inputs, bb, g)
        in_maps.append(d)
        row_maps.append(rows[:2048])
    res = run_bass_kernel_spmd(nc, in_maps, core_ids=list(range(8)))
    out = np.empty((4, TSEQ, DM), dtype=np.float32)
    for core in range(8):
        bb = core // 2
        out[bb, row_maps[core]] = np.asarray(res.results[core]["out"], dtype=np.float32)
    return out
